# version 33
# baseline (speedup 1.0000x reference)
import numpy as np
import ml_dtypes
from contextlib import ExitStack

import concourse.bass as bass
import concourse.mybir as mybir
from concourse.bass_utils import run_bass_kernel_spmd

F32 = mybir.dt.float32
BF16 = mybir.dt.bfloat16
AF = mybir.ActivationFunctionType
ALU = mybir.AluOpType

NCORES = 8
D = 1024
CC = 8
NH = 16
DH = 64

C_SUM, C_SQ, C_RSTD, C_NM = 0, 16, 32, 48
XA_MEAN, XA_VAR, XA_MSQ = 64, 66, 68
B_SUM, B_SQ, B_RSTD, B_NM = 70, 72, 74, 76
XB_MEAN, XB_VAR, XB_MSQ = XA_MEAN, XA_VAR, XA_MSQ
C_BQ, C_BK, C_MUR, C_MUK, C_OMR, C_OMK, C_RCP, C_EPS = 78, 86, 94, 102, 110, 118, 126, 142
CONST_W = 143
CB_I, CB_MASK, CB_UCOL, CB_ONES1, CB_BVROW = 0, 128, 256, 257, 385
CB_W = 1409


class Builder:
    ENGINES = ("sync", "gpsimd", "tensor", "scalar", "vector")
    SEM_OWNER = {"P": "tensor", "A": "scalar", "V": "vector", "G": "gpsimd"}

    def __init__(self):
        self.streams = {e: [] for e in self.ENGINES}
        self.cnt = {s: 0 for s in ("dC", "dX0", "dX1", "dXa", "dXb", "dXc", "dXd",
                                   "dXe", "dXf", "dXg", "dXh",
                                   "dY0", "dY1", "dY2", "dY3", "dYr0", "dYr1",
                                   "dO0", "dO1", "dO2", "dO3", "dW0", "dW1", "dW2", "P", "A", "V", "G")}
        self.lastw = {e: {} for e in self.ENGINES}

    AUTO_CHAIN = {"vector": "V", "scalar": "A", "gpsimd": "G"}

    def emit(self, eng, fn, waits=(), inc=None, amt=1):
        st = self.streams[eng]
        waits = [t for t in waits if t is not None]
        own = self.AUTO_CHAIN.get(eng)
        if own is not None:
            waits.append((own, self.cnt[own]))
            if inc is None:
                inc = own
        for s, v in waits:
            if v <= 0:
                continue
            assert v <= self.cnt[s], f"wait {s}>={v} scheduled before inc ({self.cnt[s]})"
            if self.lastw[eng].get(s, 0) >= v:
                continue
            self.lastw[eng][s] = v
            st.append(("w", s, v))
        tok = None
        if inc is not None:
            self.cnt[inc] += amt
            tok = (inc, self.cnt[inc])
        st.append(("o", fn, inc, amt))
        return tok

    def wait_only(self, eng, waits):
        for t in waits:
            if t is None:
                continue
            s, v = t
            if v <= 0:
                continue
            assert v <= self.cnt[s]
            if self.lastw[eng].get(s, 0) >= v:
                continue
            self.lastw[eng][s] = v
            self.streams[eng].append(("w", s, v))

    def replay(self, eng_name, eng, sems):
        for item in self.streams[eng_name]:
            if item[0] == "w":
                eng.wait_ge(sems[item[1]], item[2])
            else:
                _, fn, inc, amt = item
                ins = fn(eng)
                if inc is not None:
                    ins.then_inc(sems[inc], amt)


def chunks8(tc_count):
    return [list(range(g, min(g + 8, tc_count))) for g in range(0, tc_count, 8)]


def build_nc(nseq, S, dbg=None):
    TC = S // 128
    KI = S // 128
    NQB = S // 512
    FS = 8 * S
    NSL = min(4, FS // 2048)
    assert S % 512 == 0 and S >= 512

    nc = bass.Bass("TRN2", target_bir_lowering=False, debug=False)
    b = Builder()

    xd = nc.dram_tensor("x", [nseq, S, D], F32, kind="ExternalInput")
    wq = nc.dram_tensor("wqt", [D, D], F32, kind="ExternalInput")
    wk = nc.dram_tensor("wkt", [D, D], F32, kind="ExternalInput")
    wv = nc.dram_tensor("wvt", [D, D], F32, kind="ExternalInput")
    wr = nc.dram_tensor("wrt", [D, D], F32, kind="ExternalInput")
    wkf = nc.dram_tensor("wkft", [D, D], F32, kind="ExternalInput")
    wvf = nc.dram_tensor("wvft", [D, D], F32, kind="ExternalInput")
    bqd = nc.dram_tensor("bq", [CC, 128, 1], F32, kind="ExternalInput")
    bkd = nc.dram_tensor("bk", [CC, 128, 1], F32, kind="ExternalInput")
    bvd = nc.dram_tensor("bv", [1, D], F32, kind="ExternalInput")
    murd = nc.dram_tensor("mur", [CC, 128, 1], F32, kind="ExternalInput")
    mukd = nc.dram_tensor("muk", [CC, 128, 1], F32, kind="ExternalInput")
    identd = nc.dram_tensor("ident", [128, 128], BF16, kind="ExternalInput")
    maskd = nc.dram_tensor("maskt", [128, 128], BF16, kind="ExternalInput")
    outd = nc.dram_tensor("out", [nseq, S, D], F32, kind="ExternalOutput")
    yscr = nc.dram_tensor("yscr", [S, D], F32, kind="Internal")
    if dbg:
        dbg1 = nc.dram_tensor("dbg1", [128, 8 * S], BF16, kind="ExternalOutput")
        dbg2 = nc.dram_tensor("dbg2", [128, 8 * S], BF16, kind="ExternalOutput")
        dbg3 = nc.dram_tensor("dbg3", [128, (S // 128) * 1040], BF16, kind="ExternalOutput")

    with ExitStack() as ctx:
        def sbuf(name, shape, dt):
            return ctx.enter_context(nc.sbuf_tensor(name, shape, dt))

        W0 = sbuf("W0", [128, 8192], BF16)
        W1 = sbuf("W1", [128, 8192], BF16)
        S1 = sbuf("S1", [128, FS], BF16)
        S2 = sbuf("S2", [128, FS], BF16)
        S3 = sbuf("S3", [128, FS], BF16)
        S4 = sbuf("S4", [128, TC * 1040], BF16)
        S5 = sbuf("S5", [128, FS], BF16)
        POOL = sbuf("POOL", [128, 3072], F32)
        CONST = sbuf("CONST", [128, CONST_W], F32)
        CB = sbuf("CB", [128, CB_W], BF16)
        psA = ctx.enter_context(nc.psum_tensor("psA", [128, 2048], F32))
        psB = ctx.enter_context(nc.psum_tensor("psB", [128, 2048], F32))

        sems = {}
        for s in b.cnt:
            sems[s] = ctx.enter_context(nc.semaphore("sem_" + s))

        def PT(q):
            return S1[:, 1024 * q:1024 * q + 1024]

        NPT = min(4, FS // 2048 - 1)
        OSB_LO = 1024 * NPT
        NT_LO = FS - 2048
        SCR = S4[:, 0:32].bitcast(F32)

        E = b.emit

        def pair_ap(i):
            t = psA if i < 2 else psB
            return t[:, 1024 * (i % 2):1024 * (i % 2) + 1024]

        E("sync", lambda e: e.dma_start(CB[:, CB_I:CB_I + 128], identd[:, :]), inc="dC", amt=16)
        E("sync", lambda e: e.dma_start(CB[:, CB_MASK:CB_MASK + 128], maskd[:, :]), inc="dC", amt=16)
        for cj in range(CC):
            E("sync", lambda e, cj=cj: e.dma_start(CONST[:, C_BQ + cj:C_BQ + cj + 1], bqd[cj]), inc="dC", amt=16)
            E("sync", lambda e, cj=cj: e.dma_start(CONST[:, C_BK + cj:C_BK + cj + 1], bkd[cj]), inc="dC", amt=16)
            E("sync", lambda e, cj=cj: e.dma_start(CONST[:, C_MUR + cj:C_MUR + cj + 1], murd[cj]), inc="dC", amt=16)
            E("sync", lambda e, cj=cj: e.dma_start(CONST[:, C_MUK + cj:C_MUK + cj + 1], mukd[cj]), inc="dC", amt=16)
        t_bvrow = E("sync", lambda e: e.dma_start(POOL[0:1, 0:1024], bvd[:, :]), inc="dC", amt=16)
        tok_consts = ("dC", b.cnt["dC"])

        t_w0pre = E("gpsimd", lambda e: e.dma_start(
            W0[:, :], bass.AP(wv, 0, [[1024, 128], [131072, 8], [1, 1024]])), inc="dW0", amt=16)
        t_w1pre = E("gpsimd", lambda e: e.dma_start(
            W1[:, :], bass.AP(wk, 0, [[1024, 128], [131072, 8], [1, 1024]])), inc="dW1", amt=16)

        E("vector", lambda e: e.tensor_scalar(
            CONST[:, C_OMR:C_OMR + 8], CONST[:, C_MUR:C_MUR + 8], -1.0, 1.0,
            ALU.mult, ALU.add), waits=[tok_consts], inc="V")
        E("vector", lambda e: e.tensor_scalar(
            CONST[:, C_OMK:C_OMK + 8], CONST[:, C_MUK:C_MUK + 8], -1.0, 1.0,
            ALU.mult, ALU.add), inc="V")
        E("vector", lambda e: e.memset(CB[:, CB_UCOL:CB_UCOL + 1], 1.0 / S), inc="V")
        t_eps = E("vector", lambda e: e.memset(CONST[:, C_EPS:C_EPS + 1], 1e-5), inc="V")
        t_ones1 = E("vector", lambda e: e.memset(CB[0:1, CB_ONES1:CB_ONES1 + 128], 1.0), inc="V")
        t_bvrowbf = E("vector", lambda e: e.tensor_copy(
            CB[0:1, CB_BVROW:CB_BVROW + 1024], POOL[0:1, 0:1024]),
            waits=[tok_consts], inc="V")

        ps_free = {"psA": [], "psB": []}
        pool_state = [[t_bvrowbf], [], []]
        prev_attn_P_done = None
        prev_mix_done = None
        yreads_done = []
        prev_final_reads = []
        s5x_state = [[], [], [], []]
        chain_state = {"X": []}

        def ln_chain(cset, lo, width, stat_toks, sum_ap=None):
            cs = CONST
            if cset == "A":
                c_sum, c_sq, c_rstd, c_nm = C_SUM + lo, C_SQ + lo, C_RSTD + lo, C_NM + lo
                c_mean, c_var, c_msq = XA_MEAN, XA_VAR, XA_MSQ
            else:
                c_sum, c_sq, c_rstd, c_nm = B_SUM + lo, B_SQ + lo, B_RSTD + lo, B_NM + lo
                c_mean, c_var, c_msq = XB_MEAN + lo, XB_VAR + lo, XB_MSQ + lo
            sl = lambda c: cs[:, c:c + width]
            s_ap = sl(c_sum) if sum_ap is None else sum_ap
            E("vector", lambda e: e.tensor_scalar(
                sl(c_mean), s_ap, 1.0 / D, None, ALU.mult),
                waits=list(stat_toks) + chain_state["X"], inc="V")
            E("vector", lambda e: e.tensor_tensor(sl(c_msq), sl(c_mean), sl(c_mean), ALU.mult), inc="V")
            tv = E("vector", lambda e: e.scalar_tensor_tensor(
                sl(c_var), sl(c_sq), 1.0 / D, sl(c_msq), ALU.mult, ALU.subtract), inc="V")
            E("scalar", lambda e: e.activation(sl(c_var), sl(c_var), AF.Ln,
                                               bias=CONST[:, C_EPS:C_EPS + 1]),
              waits=[tv, t_eps], inc="A")
            ta = E("scalar", lambda e: e.activation(sl(c_rstd), sl(c_var), AF.Exp, scale=-0.5),
                   inc="A")
            tn = E("vector", lambda e: e.tensor_scalar(
                sl(c_nm), sl(c_mean), -1.0, None, ALU.mult), inc="V")
            chain_state["X"] = [ta, tn]
            return ta, tn

        def stat_pair(src_ap, sum_col, sq_col, dumpA, dumpV, ready, extra=()):
            tvs = E("scalar", lambda e: e.activation(
                dumpV, src_ap, AF.Identity,
                accum_out=CONST[:, sum_col:sum_col + 1]),
                waits=[ready] + list(extra), inc="A")
            tas = E("scalar", lambda e: e.activation(
                dumpA, src_ap, AF.Square,
                accum_out=CONST[:, sq_col:sq_col + 1]),
                waits=[ready] + list(extra), inc="A")
            return tvs, tas

        for n in range(nseq):
            if dbg == f"end@{n - 1}":
                break
            ntile_state = [[], []]
            evac_toks = []
            p1_pstate = {0: ps_free["psA"], 1: ps_free["psA"],
                         2: ps_free["psB"], 3: ps_free["psB"]}
            for t0 in range(0, TC, 2):
                stat_toks = []
                xt = {}
                for tc in (t0, t0 + 1):
                    slot = tc % NSL
                    src = S5[:, 2048 * slot:2048 * slot + 2048].bitcast(F32)
                    tok = E("sync", lambda e, tc=tc, slot=slot, n=n, src=src: e.dma_start(
                        src, xd[n, 128 * tc:128 * tc + 128, :]),
                        waits=s5x_state[slot], inc=("dXa", "dXb", "dXc", "dXd")[slot], amt=16)
                    xt[tc] = (src, slot)
                    tvs, tas = stat_pair(src, C_SUM + tc, C_SQ + tc, S2[:, 0:1024], S2[:, 1024:2048], tok)
                    stat_toks += [tvs, tas]
                ta_r, _ = ln_chain("A", t0, 2, stat_toks)
                for tc in (t0, t0 + 1):
                    src, slot = xt[tc]
                    nt = S3[:, NT_LO + 1024 * (tc % 2):NT_LO + 1024 * (tc % 2) + 1024]
                    mixw = ([prev_mix_done] + prev_final_reads) if (n > 0 and tc <= 1) else []
                    ta = E("vector", lambda e, src=src, nt=nt, tc=tc: e.tensor_scalar(
                        nt, src,
                        CONST[:, C_NM + tc:C_NM + tc + 1],
                        CONST[:, C_RSTD + tc:C_RSTD + tc + 1],
                        ALU.add, ALU.mult),
                        waits=[ta_r] + ntile_state[tc % 2] + mixw, inc="V")
                    s5x_state[slot] = [ta]
                    si = tc % 4
                    tp = None
                    for cj in range(CC):
                        tp = E("tensor", lambda e, si=si, nt=nt, cj=cj: e.matmul(
                            pair_ap(si)[:, 128 * cj:128 * cj + 128],
                            nt[:, 128 * cj:128 * cj + 128],
                            CB[:, CB_I:CB_I + 128], start=True, stop=True),
                            waits=([ta] + p1_pstate[si]) if cj == 0 else (),
                            inc="P" if cj == CC - 1 else None)
                    ntile_state[tc % 2] = [tp]
                    tv = E("vector", lambda e, si=si, tc=tc: e.tensor_copy(
                        bass.AP(S1, 128 * tc, [[FS, 128], [S, CC], [1, 128]]),
                        pair_ap(si)[:, 0:1024]),
                        waits=[tp], inc="V")
                    p1_pstate[si] = [tv]
                    evac_toks.append(tv)
            xn1_tok = evac_toks[-1]
            ps_free["psA"] = p1_pstate[0] + p1_pstate[1]
            ps_free["psB"] = p1_pstate[2] + p1_pstate[3]

            if dbg == f"xn1@{n}":
                E("sync", lambda e: e.dma_start(dbg1[:, :], S1[:, :]),
                  waits=[xn1_tok], inc="dO", amt=16)
                break
            def proj(Wt, src, evac_fn, first_waits, cj_toks=None):
                NHB = 2 if NQB >= 2 else 1
                HQB = NQB // NHB
                HW = 512 * HQB
                evac = {}
                tmm = None
                for dj in range(CC):
                    base = psA if dj % 2 == 0 else psB
                    for hb in range(NHB):
                        ps = base[:, HW * hb:HW * hb + HW]
                        if dj < 2:
                            w = list(first_waits) + ps_free["psA" if dj % 2 == 0 else "psB"]
                        else:
                            w = [evac[(dj - 2, hb)]]
                        for cj in range(CC):
                            for qb in range(HQB):
                                last = (cj == CC - 1 and qb == HQB - 1)
                                waits = w if (cj == 0 and qb == 0) else ()
                                if cj_toks is not None and dj == 0 and qb == 0:
                                    waits = list(waits) + [cj_toks[cj]]
                                tmm = E("tensor", lambda e, ps=ps, dj=dj, cj=cj, qb=qb, hb=hb, HW=HW, Wt=Wt, src=src: e.matmul(
                                    ps[:, 512 * qb:512 * qb + 512],
                                    Wt[:, 1024 * cj + 128 * dj:1024 * cj + 128 * dj + 128],
                                    src[:, S * cj + HW * hb + 512 * qb:S * cj + HW * hb + 512 * qb + 512],
                                    start=(cj == 0), stop=(cj == CC - 1)),
                                    waits=waits,
                                    inc="P" if last else None)
                        evac[(dj, hb)] = evac_fn(dj, hb, HW, ps, tmm)
                ps_free["psA"] = [evac[(CC - 2, NHB - 1)]]
                ps_free["psB"] = [evac[(CC - 1, NHB - 1)]]
                return tmm, {dj: evac[(dj, NHB - 1)] for dj in range(CC)}

            def evac_qk(Wdst, bcol):
                def f(dj, hb, HW, ps, ptok):
                    return E("scalar", lambda e, dj=dj, hb=hb, HW=HW, ps=ps: e.activation(
                        Wdst[:, S * dj + HW * hb:S * dj + HW * hb + HW], ps[:, 0:HW],
                        AF.Identity,
                        bias=CONST[:, bcol + dj:bcol + dj + 1]),
                        waits=[ptok, tok_consts] + prev_final_reads, inc="A")
                return f

            t_ones = E("vector", lambda e: e.memset(
                bass.AP(S4, 64, [[TC * 1040, 128], [1040, TC], [65, NH], [1, 1]]), 1.0),
                waits=[prev_attn_P_done], inc="V")

            vpair_state = {0: ps_free["psA"], 1: ps_free["psA"],
                           2: ps_free["psB"], 3: ps_free["psB"]}
            t_vev = None
            tv_mm = None
            for tc in range(TC):
                si = tc % 4
                pst = psA if si < 2 else psB
                poff = 1024 * (si % 2)
                for hf in range(2):
                    E("tensor", lambda e, si=si, hf=hf: e.matmul(
                        pair_ap(si)[:, 512 * hf:512 * hf + 512],
                        CB[0:1, CB_ONES1:CB_ONES1 + 128],
                        CB[0:1, CB_BVROW + 512 * hf:CB_BVROW + 512 * hf + 512],
                        start=True, stop=False),
                        waits=([t_ones1, t_bvrowbf] + vpair_state[si]) if hf == 0 else ())
                for cj in range(CC):
                    for hf in range(2):
                        last = (cj == CC - 1 and hf == 1)
                        tv_mm = E("tensor", lambda e, si=si, tc=tc, cj=cj, hf=hf: e.matmul(
                            pair_ap(si)[:, 512 * hf:512 * hf + 512],
                            S1[:, S * cj + 128 * tc:S * cj + 128 * tc + 128],
                            W0[:, 1024 * cj + 512 * hf:1024 * cj + 512 * hf + 512],
                            start=False, stop=(cj == CC - 1)),
                            waits=[t_w0pre, evac_toks[tc]] if (cj == 0 and hf == 0) else (),
                            inc="P" if last else None)
                t_vev = E("vector", lambda e, si=si, tc=tc, pst=pst, poff=poff: e.tensor_copy(
                    bass.AP(S4, 1040 * tc, [[TC * 1040, 128], [65, NH], [1, 64]]),
                    bass.AP(pst, poff, [[2048, 128], [64, NH], [1, 64]])),
                    waits=[tv_mm, t_ones], inc="V")
                vpair_state[si] = [t_vev]
            V_tok = t_vev
            ps_free["psA"] = vpair_state[0] + vpair_state[1]
            ps_free["psB"] = vpair_state[2] + vpair_state[3]

            t_wqa = E("gpsimd", lambda e: e.dma_start(
                W0[:, 0:4096], bass.AP(wq, 0, [[1024, 128], [131072, 4], [1, 1024]])),
                waits=[tv_mm], inc="dW0", amt=16)
            t_wqb = E("gpsimd", lambda e: e.dma_start(
                W0[:, 4096:8192], bass.AP(wq, 4 * 131072, [[1024, 128], [131072, 4], [1, 1024]])),
                waits=[tv_mm], inc="dW2", amt=16)

            tk_mm, tk_ev = proj(W1, S1, evac_qk(S3, C_BK), [xn1_tok, t_w1pre])
            KT_tok = tk_ev[CC - 1]
            t_wr = E("gpsimd", lambda e: e.dma_start(
                W1[:, :], bass.AP(wr, 0, [[1024, 128], [131072, 8], [1, 1024]])),
                waits=[tk_mm], inc="dW1", amt=16)

            tq_mm, tq_ev = proj(W0, S1, evac_qk(S2, C_BQ), [t_wqa, t_wqb])
            QT_tok = tq_ev[CC - 1]
            t_wkf = E("gpsimd", lambda e: e.dma_start(
                W0[:, :], bass.AP(wkf, 0, [[1024, 128], [131072, 8], [1, 1024]])),
                waits=[tq_mm], inc="dW0", amt=16)

            if dbg == f"qkv@{n}":
                E("sync", lambda e: e.dma_start(dbg1[:, :], S2[:, :]), waits=[QT_tok], inc="dO", amt=16)
                E("sync", lambda e: e.dma_start(dbg2[:, :], S3[:, :]), waits=[KT_tok], inc="dO", amt=16)
                E("sync", lambda e: e.dma_start(dbg3[:, :], S4[:, :]), waits=[V_tok], inc="dO", amt=16)
                break
            QW = min(1024, S)
            NQH = S // QW
            sc_state = {i: list(ps_free["psB"]) for i in range(4)}
            psA_war = list(ps_free["psA"]) + [V_tok]
            pt_state = [[] for _ in range(NPT)]
            osb_war = []
            attn_last = []
            sc_i = 0
            pt_i = 0
            last_P3_P = None
            for hp in range(NH // 2):
                ch = hp
                for qh in range(NQH):
                    qlo = QW * qh
                    qhi = qlo + QW
                    blocks = []
                    pre_us = []
                    subs_seq = []
                    for ki in range(KI):
                        ncols = 128 * (ki + 1)
                        dlo = ncols - 128
                        lo, hi = qlo, min(qhi, ncols)
                        if hi <= lo:
                            if qh == NQH - 1:
                                pre_us.append(ki)
                                subs_seq.append(("u", ki))
                            continue
                        for c0 in range(lo, hi, 512):
                            c1 = min(c0 + 512, hi)
                            subs = []
                            for cs in (c0,):
                                ce = c1
                                if cs < dlo < ce:
                                    subs.append((cs, dlo))
                                    subs.append((dlo, ce))
                                else:
                                    subs.append((cs, ce))
                            if ki == KI - 1:
                                subs = [(a, min(bb, S - 1)) for a, bb in subs if a < S - 1]
                            blk = {"ki": ki, "c0": c0, "c1": c1, "subs": subs,
                                   "diag": c0 <= dlo < c1,
                                   "u": (qh == NQH - 1 and c1 == hi)}
                            blocks.append(blk)
                            for sub in subs:
                                subs_seq.append(("pv", sub))
                            if blk["u"]:
                                subs_seq.append(("u", ki))
                    subs_seq = ([("u", ki) for ki in pre_us]
                                + [sq for sq in subs_seq if not (sq[0] == "u" and sq[1] in pre_us)])
                    first_of_bank = {}
                    last_of_bank = {}
                    for i, sq in enumerate(subs_seq):
                        bk = ((S - 1 - qlo) // 512) if sq[0] == "u" else ((sq[1][0] - qlo) // 512)
                        first_of_bank.setdefault(bk, i)
                        last_of_bank[bk] = i

                    def emit_S_pair(blk, ch=ch):
                        ki, c0, c1 = blk["ki"], blk["c0"], blk["c1"]
                        for e2 in (0, 1):
                            slot = blk["slotA"] + e2
                            rb = 64 * e2
                            waits = []
                            if e2 == 0:
                                waits = (sc_state[slot] + sc_state[slot + 1]
                                         + ([QT_tok, KT_tok] if sc_i <= 2 else []))
                            blk["tS"] = E("tensor", lambda e, slot=slot, ki=ki, c0=c0, c1=c1, ch=ch, rb=rb: e.matmul(
                                psB[:, 512 * slot:512 * slot + (c1 - c0)],
                                S3[rb:rb + 64, S * ch + 128 * ki:S * ch + 128 * ki + 128],
                                S2[rb:rb + 64, S * ch + c0:S * ch + c1],
                                start=True, stop=True),
                                waits=waits,
                                inc="P" if e2 == 1 else None)

                    def emit_E_pair(blk):
                        slotA, half = blk["slotA"], blk["half"]
                        w = blk["c1"] - blk["c0"]
                        if w == 512:
                            t_exp = E("scalar", lambda e, slotA=slotA, half=half: e.activation(
                                PT(half)[:, 0:1024], psB[:, 512 * slotA:512 * slotA + 1024],
                                AF.Exp, scale=0.25),
                                waits=[blk["tS"]] + pt_state[half], inc="A")
                        else:
                            E("scalar", lambda e, slotA=slotA, half=half, w=w: e.activation(
                                PT(half)[:, 0:w], psB[:, 512 * slotA:512 * slotA + w],
                                AF.Exp, scale=0.25),
                                waits=[blk["tS"]] + pt_state[half], inc="A")
                            t_exp = E("scalar", lambda e, slotA=slotA, half=half, w=w: e.activation(
                                PT(half)[:, 512:512 + w],
                                psB[:, 512 * slotA + 512:512 * slotA + 512 + w],
                                AF.Exp, scale=0.25), inc="A")
                        sc_state[slotA] = [t_exp]
                        sc_state[slotA + 1] = [t_exp]
                        blk["te"] = t_exp
                        if blk["diag"]:
                            c0 = blk["c0"]
                            dlo = 128 * (blk["ki"] + 1) - 128
                            tm = None
                            for e2 in (0, 1):
                                tm = E("vector", lambda e, half=half, c0=c0, dlo=dlo, e2=e2: e.tensor_tensor(
                                    PT(half)[:, 512 * e2 + dlo - c0:512 * e2 + dlo - c0 + 128],
                                    PT(half)[:, 512 * e2 + dlo - c0:512 * e2 + dlo - c0 + 128],
                                    CB[:, CB_MASK:CB_MASK + 128], ALU.mult),
                                    waits=[t_exp], inc="V" if e2 == 1 else None)
                            blk["tm"] = tm

                    def emit_PV_pair(blk, hp=hp, qlo=qlo):
                        ki, c0, half = blk["ki"], blk["c0"], blk["half"]
                        dlo = 128 * ki
                        last_pv = None
                        n_subs = len(blk["subs"]) + (1 if blk["u"] else 0)
                        for e2 in (0, 1):
                            h = 2 * hp + e2
                            si = pv_ctx["sub_i"]
                            for (cs, ce) in blk["subs"]:
                                bk = (cs - qlo) // 512
                                fb = first_of_bank[bk] == si
                                lb = last_of_bank[bk] == si
                                waits = [blk["te"]]
                                if cs <= dlo < ce:
                                    waits.append(blk.get("tm"))
                                if si == 0 and e2 == 0:
                                    waits += psA_war
                                last_pv = E("tensor", lambda e, ki=ki, h=h, e2=e2, cs=cs, ce=ce, half=half, c0=c0, fb=fb, lb=lb, qlo=qlo: e.matmul(
                                    psA[0:65, QW * e2 + (cs - qlo):QW * e2 + (ce - qlo)],
                                    S4[:, 1040 * ki + 65 * h:1040 * ki + 65 * h + 65],
                                    PT(half)[:, 512 * e2 + (cs - c0):512 * e2 + (ce - c0)],
                                    start=fb, stop=lb),
                                    waits=waits, inc="P")
                                si += 1
                            if blk["u"]:
                                bk = (S - 1 - qlo) // 512
                                fb = first_of_bank[bk] == si
                                lb = last_of_bank[bk] == si
                                last_pv = E("tensor", lambda e, ki=ki, h=h, e2=e2, fb=fb, lb=lb, qlo=qlo: e.matmul(
                                    psA[0:65, QW * e2 + (S - 1 - qlo):QW * e2 + (S - qlo)],
                                    S4[:, 1040 * ki + 65 * h:1040 * ki + 65 * h + 65],
                                    CB[:, CB_UCOL:CB_UCOL + 1],
                                    start=fb, stop=lb),
                                    waits=(), inc="P")
                                si += 1
                        pv_ctx["sub_i"] = si
                        pt_state[half] = [last_pv]
                        pv_ctx["last_pv"] = last_pv

                    pv_ctx = {"last_pv": None, "sub_i": 0}
                    for ui, ki in enumerate(pre_us):
                        si = pv_ctx["sub_i"]
                        bk = (S - 1 - qlo) // 512
                        fb = first_of_bank[bk] == si
                        lb = last_of_bank[bk] == si
                        for e2 in (0, 1):
                            h = 2 * hp + e2
                            waits = psA_war if (si == 0 and e2 == 0) else []
                            pv_ctx["last_pv"] = E("tensor", lambda e, ki=ki, h=h, e2=e2, fb=fb, lb=lb, qlo=qlo: e.matmul(
                                psA[0:65, QW * e2 + (S - 1 - qlo):QW * e2 + (S - qlo)],
                                S4[:, 1040 * ki + 65 * h:1040 * ki + 65 * h + 65],
                                CB[:, CB_UCOL:CB_UCOL + 1],
                                start=fb, stop=lb),
                                waits=waits, inc="P")
                        pv_ctx["sub_i"] = si + 1
                    pending = None
                    for blk in blocks:
                        blk["slotA"] = (sc_i % 2) * 2
                        sc_i += 1
                        blk["half"] = pt_i % NPT
                        pt_i += 1
                        emit_S_pair(blk)
                        if pending is not None:
                            emit_PV_pair(pending)
                        emit_E_pair(blk)
                        pending = blk
                    emit_PV_pair(pending)
                    last_pv = pv_ctx["last_pv"]

                    t_osb = E("vector", lambda e: e.tensor_copy(
                        S1[0:65, OSB_LO:OSB_LO + 2 * QW], psA[0:65, 0:2 * QW]),
                        waits=[last_pv] + osb_war, inc="V")
                    psA_war = [t_osb]
                    for e2 in (0, 1):
                        h = 2 * hp + e2
                        for g0 in range(0, QW // 128, 4):
                            qcs = [QW // 128 * qh + g0 + j for j in range(4)]
                            slot = sc_i % 4
                            sc_i += 1
                            tT = None
                            for j, qc in enumerate(qcs):
                                jl = g0 + j
                                tT = E("tensor", lambda e, slot=slot, e2=e2, jl=jl, j=j: e.matmul(
                                    psB[:, 512 * slot + 128 * j:512 * slot + 128 * j + 65],
                                    S1[0:65, OSB_LO + QW * e2 + 128 * jl:OSB_LO + QW * e2 + 128 * jl + 128],
                                    CB[0:65, CB_I:CB_I + 65], start=True, stop=True),
                                    waits=([t_osb] + sc_state[slot]) if j == 0 else (),
                                    inc="P" if j == len(qcs) - 1 else None)
                            osb_war = [tT]
                            last_P3_P = tT
                            t_rcp = E("vector", lambda e, slot=slot, nq=len(qcs): e.reciprocal(
                                CONST[:, C_RCP + 4 * slot:C_RCP + 4 * slot + nq],
                                bass.AP(psB, 512 * slot + 64, [[2048, 128], [128, nq], [1, 1]])),
                                waits=[tT], inc="V")
                            tnv = None
                            for j, qc in enumerate(qcs):
                                tnv = E("vector", lambda e, slot=slot, qc=qc, j=j, h=h: e.tensor_scalar(
                                    S5[:, 1024 * qc + 64 * h:1024 * qc + 64 * h + 64],
                                    psB[:, 512 * slot + 128 * j:512 * slot + 128 * j + 64],
                                    CONST[:, C_RCP + 4 * slot + j:C_RCP + 4 * slot + j + 1],
                                    None, ALU.mult),
                                    inc="V" if j == len(qcs) - 1 else None)
                            sc_state[slot] = [tnv]
                            attn_last = [tnv]
            attn_done = attn_last
            prev_attn_P_done = last_P3_P
            ps_free["psA"] = psA_war
            ps_free["psB"] = sc_state[0] + sc_state[1]

            if dbg == f"attn@{n}":
                E("sync", lambda e: e.dma_start(dbg1[:, :], S5[:, :]),
                  waits=attn_done, inc="dO", amt=16)
                break
            z_evac_toks = []
            p4_pstate = {0: list(ps_free["psA"]), 1: list(ps_free["psA"]),
                         2: list(ps_free["psB"]), 3: list(ps_free["psB"])}
            ntile_state = [[], []]
            NYB = 4 if FS >= 8192 else 2
            y_state = {k: [] for k in range(NYB)}
            y_state[0] = list(pool_state[2])
            Yap = {0: POOL[:, 2048:3072], 1: S2[:, 2048:4096].bitcast(F32)}
            if NYB == 4:
                Yap[2] = S2[:, 4096:6144].bitcast(F32)
                Yap[3] = S2[:, 6144:8192].bitcast(F32)
            p4_attn_reads = []
            tysum = E("vector", lambda e: e.tensor_copy(
                SCR[:, 0:TC], CONST[:, C_SUM:C_SUM + TC]),
                waits=attn_done, inc="V")
            NXS = 4 if FS >= 10240 else 2
            XS = {0: POOL[:, 0:1024], 1: POOL[:, 1024:2048]}
            XSTRM = {0: "dX0", 1: "dX1", 2: "dXe", 3: "dXf"}
            if NXS == 4:
                XS[2] = S3[:, 4096:6144].bitcast(F32)
                XS[3] = S3[:, 6144:8192].bitcast(F32)
            xslot_state = {0: list(pool_state[0]), 1: list(pool_state[1]),
                           2: [prev_attn_P_done, ("dXg", b.cnt["dXg"])],
                           3: [prev_attn_P_done, ("dXh", b.cnt["dXh"])]}
            xtoks = {}

            def issue_x(tc):
                slot = tc % NXS
                xtoks[tc] = E("sync", lambda e, tc=tc, slot=slot, n=n: e.dma_start(
                    XS[slot], xd[n, 128 * tc:128 * tc + 128, :]),
                    waits=xslot_state[slot], inc=XSTRM[slot], amt=16)

            if NXS == 4:
                issue_x(0)
                issue_x(1)
            for t0 in range(0, TC, 2):
                if NXS == 4:
                    if t0 + 2 < TC:
                        issue_x(t0 + 2)
                        issue_x(t0 + 3)
                else:
                    issue_x(t0)
                    issue_x(t0 + 1)
                stat_toks = []
                for tc in (t0, t0 + 1):
                    for tok in stat_pair(S5[:, 1024 * tc:1024 * tc + 1024], C_SUM + tc, C_SQ + tc,
                                         S2[:, 0:1024], S2[:, 1024:2048],
                                         attn_done[0],
                                         extra=list(attn_done[1:]) + [tysum]):
                        stat_toks.append(tok)
                        p4_attn_reads.append(tok)
                ta_r, _ = ln_chain("A", t0, 2, stat_toks)
                ystat_toks = []
                yinfo = {}
                for tc in (t0, t0 + 1):
                    p = tc % 2
                    yb = tc % NYB
                    Y = Yap[yb]
                    t1 = E("vector", lambda e, tc=tc, Y=Y: e.tensor_scalar(
                        Y, S5[:, 1024 * tc:1024 * tc + 1024],
                        CONST[:, C_NM + tc:C_NM + tc + 1],
                        CONST[:, C_RSTD + tc:C_RSTD + tc + 1],
                        ALU.add, ALU.mult),
                        waits=[ta_r] + y_state[yb], inc="V")
                    p4_attn_reads.append(t1)
                    t2 = E("vector", lambda e, Y=Y, XT=XS[tc % NXS]: e.tensor_tensor(
                        Y, Y, XT, ALU.add),
                        waits=[xtoks[tc]], inc="V")
                    xslot_state[tc % NXS] = [t2]
                    if tc % NXS in (0, 1):
                        pool_state[tc % NXS] = [t2]
                    tydma = E("sync", lambda e, tc=tc, Y=Y: e.dma_start(
                        yscr[128 * tc:128 * tc + 128, :], Y),
                        waits=[t2] + yreads_done, inc="dY%d" % (tc % 4), amt=16)
                    tas = E("scalar", lambda e, Y=Y, p=p: e.activation(
                        S3[:, 0:1024], Y, AF.Square,
                        accum_out=CONST[:, B_SQ + p:B_SQ + p + 1]),
                        waits=[t2], inc="A")
                    ystat_toks.append(tas)
                    yinfo[tc] = (Y, tydma, tas)
                tb_r, _ = ln_chain("B", 0, 2, [tysum] + ystat_toks,
                                   sum_ap=SCR[:, t0:t0 + 2])
                for tc in (t0, t0 + 1):
                    p = tc % 2
                    Y, tydma, tas = yinfo[tc]
                    nt = S3[:, NT_LO + 1024 * p:NT_LO + 1024 * p + 1024]
                    tz = E("vector", lambda e, Y=Y, nt=nt, p=p: e.tensor_scalar(
                        nt, Y,
                        CONST[:, B_NM + p:B_NM + p + 1],
                        CONST[:, B_RSTD + p:B_RSTD + p + 1],
                        ALU.add, ALU.mult),
                        waits=[tb_r] + ntile_state[p], inc="V")
                    y_state[tc % NYB] = [tz, tydma, tas]
                    si = tc % 4
                    tp = None
                    for cj in range(CC):
                        tp = E("tensor", lambda e, si=si, nt=nt, cj=cj: e.matmul(
                            pair_ap(si)[:, 128 * cj:128 * cj + 128],
                            nt[:, 128 * cj:128 * cj + 128],
                            CB[:, CB_I:CB_I + 128], start=True, stop=True),
                            waits=([tz] + p4_pstate[si]) if cj == 0 else (),
                            inc="P" if cj == CC - 1 else None)
                    ntile_state[p] = [tp]
                    tvz = E("vector", lambda e, si=si, tc=tc: e.tensor_copy(
                        bass.AP(S1, 128 * tc, [[FS, 128], [S, CC], [1, 128]]),
                        pair_ap(si)[:, 0:1024]),
                        waits=[tp], inc="V")
                    p4_pstate[si] = [tvz]
                    z_evac_toks.append(tvz)
            zT_tok = z_evac_toks[-1]
            ywrites_done = [("dY%d" % i, b.cnt["dY%d" % i]) for i in range(4)]
            pool_state[2] = [t for k in range(NYB) for t in y_state[k]]
            ps_free["psA"] = p4_pstate[0] + p4_pstate[1]
            ps_free["psB"] = p4_pstate[2] + p4_pstate[3]

            if dbg == f"z@{n}":
                E("sync", lambda e: e.dma_start(dbg1[:, :], S1[:, :]),
                  waits=[zT_tok], inc="dO", amt=16)
                break
            if dbg == f"ffn@{n}":
                pass
            dst_toks = {}
            xr_cc_toks = []
            tmp_state = [[], []]
            ci = 0
            for (dst, mucol, omcol) in ((S2, C_MUR, C_OMR), (S3, C_MUK, C_OMK)):
                tmix = None
                for cc in range(CC):
                    lo = S * cc
                    tb = ci % 2
                    ci += 1
                    tmp = POOL[:, 1024 * tb:1024 * tb + 1024].bitcast(BF16)
                    w = list(tmp_state[tb])
                    if cc == 0 and dst is S2:
                        w += ([zT_tok, prev_attn_P_done] + pool_state[0]
                              + pool_state[1] + pool_state[2] + ywrites_done)
                    if dst is S2:
                        ttmp = E("scalar", lambda e, lo=lo, omcol=omcol, cc=cc, tmp=tmp: e.activation(
                            tmp[:, 0:S - 1], S1[:, lo:lo + S - 1], AF.Identity,
                            scale=CONST[:, omcol + cc:omcol + cc + 1]),
                            waits=w, inc="A")
                    else:
                        ttmp = E("vector", lambda e, lo=lo, mucol=mucol, cc=cc, tmp=tmp: e.tensor_scalar(
                            tmp[:, 0:S - 1], S1[:, lo:lo + S - 1],
                            CONST[:, omcol + cc:omcol + cc + 1], None, ALU.mult),
                            waits=w, inc="V")
                    tst = E("vector", lambda e, lo=lo, dst=dst, mucol=mucol, cc=cc, tmp=tmp: e.scalar_tensor_tensor(
                        dst[:, lo + 1:lo + S], S1[:, lo + 1:lo + S],
                        CONST[:, mucol + cc:mucol + cc + 1],
                        tmp[:, 0:S - 1], ALU.mult, ALU.add),
                        waits=[ttmp], inc="V")
                    tmix = E("vector", lambda e, lo=lo, dst=dst, mucol=mucol, cc=cc: e.tensor_scalar(
                        dst[:, lo:lo + 1], S1[:, lo:lo + 1],
                        CONST[:, mucol + cc:mucol + cc + 1], None, ALU.mult),
                        inc="V")
                    tmp_state[tb] = [tst]
                    if dst is S2:
                        xr_cc_toks.append(tmix)
                dst_toks[id(dst)] = tmix
            xr_tok = dst_toks[id(S2)]
            xk_tok = dst_toks[id(S3)]
            mix_tok = xk_tok
            prev_mix_done = mix_tok
            pool_state[0] = [xk_tok]

            def evac_r(dj, hb, HW, ps, ptok):
                return E("scalar", lambda e, dj=dj, hb=hb, HW=HW, ps=ps: e.activation(
                    S4[:, S * dj + HW * hb:S * dj + HW * hb + HW], ps[:, 0:HW],
                    AF.Tanh, scale=0.5),
                    waits=[ptok, V_tok], inc="A")

            tr_mm, _ = proj(W1, S2, evac_r, [xr_cc_toks[0], t_wr], cj_toks=xr_cc_toks)
            t_wvf = E("gpsimd", lambda e: e.dma_start(
                W1[:, :], bass.AP(wvf, 0, [[1024, 128], [131072, 8], [1, 1024]])),
                waits=[tr_mm], inc="dW1", amt=16)

            krelu_state = {}

            def evac_k(dj, hb, HW, ps, ptok):
                kb = dj % 2
                krelu = POOL[:, 1024 * kb:1024 * kb + 1024].bitcast(BF16)
                kr = krelu[:, HW * hb:HW * hb + HW]
                tkr = E("vector", lambda e, ps=ps, kr=kr, HW=HW: e.tensor_scalar(
                    kr, ps[:, 0:HW], 0.0, 0.70710678118654755,
                    ALU.max, ALU.mult),
                    waits=[ptok] + krelu_state.get((kb, hb), [])
                    + (p4_attn_reads[-3:] if dj < 2 else []),
                    inc="V")
                tsq = E("scalar", lambda e, dj=dj, hb=hb, HW=HW, kr=kr: e.activation(
                    S5[:, S * dj + HW * hb:S * dj + HW * hb + HW], kr, AF.Square),
                    waits=[tkr], inc="A")
                krelu_state[(kb, hb)] = [tsq]
                return tkr

            tk2_mm, _ = proj(W0, S3, evac_k, [xk_tok, t_wkf])
            kact_done = [t for v in krelu_state.values() for t in v]
            pool_state[0] = [t for (kb, hb), v in krelu_state.items() if kb == 0 for t in v]
            pool_state[1] = [t for (kb, hb), v in krelu_state.items() if kb == 1 for t in v]

            ft_state = [[], []]
            ffn_ev = {}
            vp_last = None
            for dj in range(CC):
                ps = psA if dj % 2 == 0 else psB
                if dj < 2:
                    w = [t_wvf] + kact_done + ps_free["psA" if dj % 2 == 0 else "psB"]
                else:
                    w = [ffn_ev[dj - 2]]
                tmm = None
                for cj in range(CC):
                    for qb in range(NQB):
                        last = (cj == CC - 1 and qb == NQB - 1)
                        tmm = E("tensor", lambda e, ps=ps, dj=dj, cj=cj, qb=qb: e.matmul(
                            ps[:, 512 * qb:512 * qb + 512],
                            W1[:, 1024 * cj + 128 * dj:1024 * cj + 128 * dj + 128],
                            S5[:, S * cj + 512 * qb:S * cj + 512 * qb + 512],
                            start=(cj == 0), stop=(cj == CC - 1)),
                            waits=w if (cj == 0 and qb == 0) else (),
                            inc="P" if last else None)
                vp_last = tmm
                ft = dj % 2
                tmul = E("vector", lambda e, dj=dj, ps=ps, ft=ft: e.scalar_tensor_tensor(
                    W0[:, 2048 * ft:2048 * ft + S], S4[:, S * dj:S * dj + S], 1.0,
                    ps[:, 0:S], ALU.add, ALU.mult),
                    waits=[tmm] + ft_state[ft], inc="V")
                tT = None
                evs = []
                for g, tcs in enumerate(chunks8(TC)):
                    for j, tc in enumerate(tcs):
                        tT = E("tensor", lambda e, ps=ps, ft=ft, tc=tc, g=g, j=j: e.matmul(
                            ps[:, 1024 * g + 128 * j:1024 * g + 128 * j + 128],
                            W0[:, 2048 * ft + 128 * tc:2048 * ft + 128 * tc + 128],
                            CB[:, CB_I:CB_I + 128], start=True, stop=True),
                            waits=[tmul] if (g == 0 and j == 0) else (),
                            inc="P" if j == len(tcs) - 1 else None)
                    tev = E("vector", lambda e, ps=ps, dj=dj, g=g, ntc=len(tcs): e.tensor_copy(
                        bass.AP(S2, 1024 * 8 * g + 128 * dj,
                                [[FS, 128], [1024, ntc], [1, 128]]),
                        ps[:, 1024 * g:1024 * g + 128 * ntc]),
                        waits=[tT], inc="V")
                    evs.append(tev)
                ft_state[ft] = [tT]
                ffn_ev[dj] = evs[-1]
            ps_free["psA"] = [ffn_ev[CC - 2]]
            ps_free["psB"] = [ffn_ev[CC - 1]]
            ffn_done = list(ffn_ev.values())

            if n + 1 < nseq and dbg != "endnp@0":
                t_w0pre = E("gpsimd", lambda e: e.dma_start(
                    W0[:, :], bass.AP(wv, 0, [[1024, 128], [131072, 8], [1, 1024]])),
                    waits=ft_state[0] + ft_state[1], inc="dW0", amt=16)
                t_w1pre = E("gpsimd", lambda e: e.dma_start(
                    W1[:, :], bass.AP(wk, 0, [[1024, 128], [131072, 8], [1, 1024]])),
                    waits=[vp_last], inc="dW1", amt=16)

            if dbg == f"ffn@{n}":
                E("sync", lambda e: e.dma_start(dbg1[:, :], S2[:, :]),
                  waits=ffn_done, inc="dO", amt=16)
                E("sync", lambda e: e.dma_start(dbg2[:, :], S5[:, :]),
                  waits=kact_done, inc="dO", amt=16)
                E("sync", lambda e: e.dma_start(dbg3[:, 0:8 * S // 2], S4[:, 0:8 * S // 2].bitcast(BF16)[:, 0:8 * S // 2] if False else S4[:, 0:8 * S // 2]),
                  waits=ffn_done, inc="dO", amt=16)
                break
            NOB = 4 if FS >= 8192 else 2
            o_state = {k: [] for k in range(NOB)}
            o_state[0] = list(pool_state[2])
            Oap = {0: POOL[:, 2048:3072], 1: S3[:, 2048:4096].bitcast(F32)}
            if NOB == 4:
                Oap[2] = S1[:, 0:2048].bitcast(F32)
                Oap[3] = S1[:, 2048:4096].bitcast(F32)
            YSTRM = {0: "dYr0", 1: "dYr1", 2: "dXg", 3: "dXh"}
            yslot_state = {0: list(pool_state[0]), 1: list(pool_state[1]),
                           2: [tk2_mm, ("dXe", b.cnt["dXe"])] + xslot_state[2],
                           3: [tk2_mm, ("dXf", b.cnt["dXf"])] + xslot_state[3]}
            ytoks = {}

            def issue_y(tc):
                slot = tc % NXS
                ytoks[tc] = E("sync", lambda e, tc=tc, slot=slot: e.dma_start(
                    XS[slot], yscr[128 * tc:128 * tc + 128, :]),
                    waits=yslot_state[slot] + (ywrites_done if tc < NXS else []),
                    inc=YSTRM[slot], amt=16)

            if NXS == 4:
                issue_y(0)
                issue_y(1)
            for t0 in range(0, TC, 2):
                if NXS == 4:
                    if t0 + 2 < TC:
                        issue_y(t0 + 2)
                        issue_y(t0 + 3)
                else:
                    issue_y(t0)
                    issue_y(t0 + 1)
                stat_toks = []
                for tc in (t0, t0 + 1):
                    tvs, tas = stat_pair(S2[:, 1024 * tc:1024 * tc + 1024], C_SUM + tc, C_SQ + tc,
                                         S3[:, 0:1024], S3[:, 1024:2048],
                                         ffn_done[-1], extra=ffn_done[:-1])
                    stat_toks += [tvs, tas]
                ta_r, _ = ln_chain("A", t0, 2, stat_toks)
                for tc in (t0, t0 + 1):
                    ob = tc % NOB
                    O = Oap[ob]
                    t1 = E("vector", lambda e, tc=tc, O=O: e.tensor_scalar(
                        O, S2[:, 1024 * tc:1024 * tc + 1024],
                        CONST[:, C_NM + tc:C_NM + tc + 1],
                        CONST[:, C_RSTD + tc:C_RSTD + tc + 1],
                        ALU.add, ALU.mult),
                        waits=[ta_r] + o_state[ob], inc="V")
                    t2 = E("vector", lambda e, O=O, YT=XS[tc % NXS]: e.tensor_tensor(
                        O, O, YT, ALU.add),
                        waits=[ytoks[tc]], inc="V")
                    yslot_state[tc % NXS] = [t2]
                    if tc % NXS in (0, 1):
                        pool_state[tc % NXS] = [t2]
                    todma = E("sync", lambda e, tc=tc, O=O, n=n: e.dma_start(
                        outd[n, 128 * tc:128 * tc + 128, :], O),
                        waits=[t2], inc="dO%d" % (tc % 4), amt=16)
                    o_state[ob] = [todma]
            pool_state[2] = list(o_state[0])
            s5x_state = [[("P", b.cnt["P"])] for _ in range(4)]
            yreads_done = [("dYr0", b.cnt["dYr0"]), ("dYr1", b.cnt["dYr1"]),
                           ("dXg", b.cnt["dXg"]), ("dXh", b.cnt["dXh"])]
            prev_final_reads = ([("V", b.cnt["V"]), ("A", b.cnt["A"])]
                                + [("dO%d" % i, b.cnt["dO%d" % i]) for i in range(4)])

        for s in ("dO0", "dO1", "dO2", "dO3", "dY0", "dY1", "dY2", "dY3",
                  "dX0", "dX1", "dXa", "dXb", "dXc", "dXd",
                  "dXe", "dXf", "dXg", "dXh", "dYr0", "dYr1", "dC"):
            b.wait_only("sync", [(s, b.cnt[s])])
        b.wait_only("gpsimd", [("dW0", b.cnt["dW0"]), ("dW1", b.cnt["dW1"]),
                               ("dW2", b.cnt["dW2"])])

        block = ctx.enter_context(nc.Block())
        block.sync(lambda e: b.replay("sync", e, sems))
        block.gpsimd(lambda e: b.replay("gpsimd", e, sems))
        block.tensor(lambda e: b.replay("tensor", e, sems))
        block.scalar(lambda e: b.replay("scalar", e, sems))
        block.vector(lambda e: b.replay("vector", e, sems))
    return nc


_NC_CACHE = {}


def get_nc(nseq, S):
    key = (nseq, S)
    if key not in _NC_CACHE:
        _NC_CACHE[key] = build_nc(nseq, S)
    return _NC_CACHE[key]


def make_in_maps(x_full, weights, nseq, S, ncores):
    bf = ml_dtypes.bfloat16
    ident = np.eye(128, dtype=bf)
    maskt = np.tril(np.ones((128, 128), dtype=np.float32), -1).astype(bf)
    shared = {
        "wqt": np.ascontiguousarray(weights["Wq"].T),
        "wkt": np.ascontiguousarray(weights["Wk"].T),
        "wvt": np.ascontiguousarray(weights["Wv"].T),
        "wrt": np.ascontiguousarray(weights["W_r"].T),
        "wkft": np.ascontiguousarray(weights["W_k"].T),
        "wvft": np.ascontiguousarray(weights["W_v"].T),
        "bq": np.ascontiguousarray(weights["bq"].reshape(CC, 128, 1)),
        "bk": np.ascontiguousarray(weights["bk"].reshape(CC, 128, 1)),
        "bv": np.ascontiguousarray(weights["bv"].reshape(1, D)),
        "mur": np.ascontiguousarray(weights["mu_r"].reshape(CC, 128, 1)),
        "muk": np.ascontiguousarray(weights["mu_k"].reshape(CC, 128, 1)),
        "ident": ident,
        "maskt": maskt,
    }
    in_maps = []
    for c in range(ncores):
        m = dict(shared)
        m["x"] = np.ascontiguousarray(x_full[c * nseq:(c + 1) * nseq])
        in_maps.append(m)
    return in_maps


def kernel(L0_attn_x, L0_attn_Wq, L0_attn_bq, L0_attn_Wk, L0_attn_bk,
           L0_attn_Wv, L0_attn_bv, L0_ffn_W_r, L0_ffn_W_k, L0_ffn_W_v,
           L0_ffn_mu_r, L0_ffn_mu_k):
    x = np.asarray(L0_attn_x, dtype=np.float32)
    B, S, D_ = x.shape
    assert D_ == D and B % NCORES == 0
    nseq = B // NCORES
    weights = {
        "Wq": np.asarray(L0_attn_Wq, np.float32), "bq": np.asarray(L0_attn_bq, np.float32),
        "Wk": np.asarray(L0_attn_Wk, np.float32), "bk": np.asarray(L0_attn_bk, np.float32),
        "Wv": np.asarray(L0_attn_Wv, np.float32), "bv": np.asarray(L0_attn_bv, np.float32),
        "W_r": np.asarray(L0_ffn_W_r, np.float32), "W_k": np.asarray(L0_ffn_W_k, np.float32),
        "W_v": np.asarray(L0_ffn_W_v, np.float32),
        "mu_r": np.asarray(L0_ffn_mu_r, np.float32), "mu_k": np.asarray(L0_ffn_mu_k, np.float32),
    }
    nc = get_nc(nseq, S)
    in_maps = make_in_maps(x, weights, nseq, S, NCORES)
    res = run_bass_kernel_spmd(nc, in_maps, core_ids=list(range(NCORES)))
    out = np.concatenate([np.asarray(r["out"], np.float32) for r in res.results], axis=0)
    return out.reshape(B, S, D)


# version 34 (final)
# speedup vs baseline: 1.0008x; 1.0008x over previous
import numpy as np
import ml_dtypes
from contextlib import ExitStack

import concourse.bass as bass
import concourse.mybir as mybir
from concourse.bass_utils import run_bass_kernel_spmd

F32 = mybir.dt.float32
BF16 = mybir.dt.bfloat16
AF = mybir.ActivationFunctionType
ALU = mybir.AluOpType

NCORES = 8
D = 1024
CC = 8
NH = 16
DH = 64

C_SUM, C_SQ, C_RSTD, C_NM = 0, 16, 32, 48
XA_MEAN, XA_VAR, XA_MSQ = 64, 66, 68
B_SUM, B_SQ, B_RSTD, B_NM = 70, 72, 74, 76
XB_MEAN, XB_VAR, XB_MSQ = XA_MEAN, XA_VAR, XA_MSQ
C_BQ, C_BK, C_MUR, C_MUK, C_OMR, C_OMK, C_RCP, C_EPS = 78, 86, 94, 102, 110, 118, 126, 142
CONST_W = 143
CB_I, CB_MASK, CB_UCOL, CB_ONES1, CB_BVROW = 0, 128, 256, 257, 385
CB_W = 1409


class Builder:
    ENGINES = ("sync", "gpsimd", "tensor", "scalar", "vector")
    SEM_OWNER = {"P": "tensor", "A": "scalar", "V": "vector", "G": "gpsimd"}

    def __init__(self):
        self.streams = {e: [] for e in self.ENGINES}
        self.cnt = {s: 0 for s in ("dC", "dX0", "dX1", "dXa", "dXb", "dXc", "dXd",
                                   "dXe", "dXf", "dXg", "dXh",
                                   "dY0", "dY1", "dY2", "dY3", "dYr0", "dYr1",
                                   "dO0", "dO1", "dO2", "dO3", "dW0", "dW1", "dW2", "P", "A", "V", "G")}
        self.lastw = {e: {} for e in self.ENGINES}

    AUTO_CHAIN = {"vector": "V", "scalar": "A", "gpsimd": "G"}

    def emit(self, eng, fn, waits=(), inc=None, amt=1):
        st = self.streams[eng]
        waits = [t for t in waits if t is not None]
        own = self.AUTO_CHAIN.get(eng)
        if own is not None:
            waits.append((own, self.cnt[own]))
            if inc is None:
                inc = own
        for s, v in waits:
            if v <= 0:
                continue
            assert v <= self.cnt[s], f"wait {s}>={v} scheduled before inc ({self.cnt[s]})"
            if self.lastw[eng].get(s, 0) >= v:
                continue
            self.lastw[eng][s] = v
            st.append(("w", s, v))
        tok = None
        if inc is not None:
            self.cnt[inc] += amt
            tok = (inc, self.cnt[inc])
        st.append(("o", fn, inc, amt))
        return tok

    def wait_only(self, eng, waits):
        for t in waits:
            if t is None:
                continue
            s, v = t
            if v <= 0:
                continue
            assert v <= self.cnt[s]
            if self.lastw[eng].get(s, 0) >= v:
                continue
            self.lastw[eng][s] = v
            self.streams[eng].append(("w", s, v))

    def replay(self, eng_name, eng, sems):
        for item in self.streams[eng_name]:
            if item[0] == "w":
                eng.wait_ge(sems[item[1]], item[2])
            else:
                _, fn, inc, amt = item
                ins = fn(eng)
                if inc is not None:
                    ins.then_inc(sems[inc], amt)


def chunks8(tc_count):
    return [list(range(g, min(g + 8, tc_count))) for g in range(0, tc_count, 8)]


def build_nc(nseq, S, dbg=None):
    TC = S // 128
    KI = S // 128
    NQB = S // 512
    FS = 8 * S
    NSL = min(8, FS // 2048)
    assert S % 512 == 0 and S >= 512

    nc = bass.Bass("TRN2", target_bir_lowering=False, debug=False)
    b = Builder()

    xd = nc.dram_tensor("x", [nseq, S, D], F32, kind="ExternalInput")
    wq = nc.dram_tensor("wqt", [D, D], F32, kind="ExternalInput")
    wk = nc.dram_tensor("wkt", [D, D], F32, kind="ExternalInput")
    wv = nc.dram_tensor("wvt", [D, D], F32, kind="ExternalInput")
    wr = nc.dram_tensor("wrt", [D, D], F32, kind="ExternalInput")
    wkf = nc.dram_tensor("wkft", [D, D], F32, kind="ExternalInput")
    wvf = nc.dram_tensor("wvft", [D, D], F32, kind="ExternalInput")
    bqd = nc.dram_tensor("bq", [CC, 128, 1], F32, kind="ExternalInput")
    bkd = nc.dram_tensor("bk", [CC, 128, 1], F32, kind="ExternalInput")
    bvd = nc.dram_tensor("bv", [1, D], F32, kind="ExternalInput")
    murd = nc.dram_tensor("mur", [CC, 128, 1], F32, kind="ExternalInput")
    mukd = nc.dram_tensor("muk", [CC, 128, 1], F32, kind="ExternalInput")
    identd = nc.dram_tensor("ident", [128, 128], BF16, kind="ExternalInput")
    maskd = nc.dram_tensor("maskt", [128, 128], BF16, kind="ExternalInput")
    outd = nc.dram_tensor("out", [nseq, S, D], F32, kind="ExternalOutput")
    yscr = nc.dram_tensor("yscr", [S, D], F32, kind="Internal")
    if dbg:
        dbg1 = nc.dram_tensor("dbg1", [128, 8 * S], BF16, kind="ExternalOutput")
        dbg2 = nc.dram_tensor("dbg2", [128, 8 * S], BF16, kind="ExternalOutput")
        dbg3 = nc.dram_tensor("dbg3", [128, (S // 128) * 1040], BF16, kind="ExternalOutput")

    with ExitStack() as ctx:
        def sbuf(name, shape, dt):
            return ctx.enter_context(nc.sbuf_tensor(name, shape, dt))

        W0 = sbuf("W0", [128, 8192], BF16)
        W1 = sbuf("W1", [128, 8192], BF16)
        S1 = sbuf("S1", [128, FS], BF16)
        S2 = sbuf("S2", [128, FS], BF16)
        S3 = sbuf("S3", [128, FS], BF16)
        S4 = sbuf("S4", [128, TC * 1040], BF16)
        S5 = sbuf("S5", [128, FS], BF16)
        POOL = sbuf("POOL", [128, 3072], F32)
        CONST = sbuf("CONST", [128, CONST_W], F32)
        CB = sbuf("CB", [128, CB_W], BF16)
        psA = ctx.enter_context(nc.psum_tensor("psA", [128, 2048], F32))
        psB = ctx.enter_context(nc.psum_tensor("psB", [128, 2048], F32))

        sems = {}
        for s in b.cnt:
            sems[s] = ctx.enter_context(nc.semaphore("sem_" + s))

        def PT(q):
            return S1[:, 1024 * q:1024 * q + 1024]

        NPT = min(4, FS // 2048 - 1)
        OSB_LO = 1024 * NPT
        NT_LO = FS - 2048
        SCR = S4[:, 0:32].bitcast(F32)

        E = b.emit

        def pair_ap(i):
            t = psA if i < 2 else psB
            return t[:, 1024 * (i % 2):1024 * (i % 2) + 1024]

        E("sync", lambda e: e.dma_start(CB[:, CB_I:CB_I + 128], identd[:, :]), inc="dC", amt=16)
        E("sync", lambda e: e.dma_start(CB[:, CB_MASK:CB_MASK + 128], maskd[:, :]), inc="dC", amt=16)
        for cj in range(CC):
            E("sync", lambda e, cj=cj: e.dma_start(CONST[:, C_BQ + cj:C_BQ + cj + 1], bqd[cj]), inc="dC", amt=16)
            E("sync", lambda e, cj=cj: e.dma_start(CONST[:, C_BK + cj:C_BK + cj + 1], bkd[cj]), inc="dC", amt=16)
            E("sync", lambda e, cj=cj: e.dma_start(CONST[:, C_MUR + cj:C_MUR + cj + 1], murd[cj]), inc="dC", amt=16)
            E("sync", lambda e, cj=cj: e.dma_start(CONST[:, C_MUK + cj:C_MUK + cj + 1], mukd[cj]), inc="dC", amt=16)
        t_bvrow = E("sync", lambda e: e.dma_start(POOL[0:1, 0:1024], bvd[:, :]), inc="dC", amt=16)
        tok_consts = ("dC", b.cnt["dC"])

        t_w0pre = E("gpsimd", lambda e: e.dma_start(
            W0[:, :], bass.AP(wv, 0, [[1024, 128], [131072, 8], [1, 1024]])), inc="dW0", amt=16)
        t_w1pre = E("gpsimd", lambda e: e.dma_start(
            W1[:, :], bass.AP(wk, 0, [[1024, 128], [131072, 8], [1, 1024]])), inc="dW1", amt=16)

        E("vector", lambda e: e.tensor_scalar(
            CONST[:, C_OMR:C_OMR + 8], CONST[:, C_MUR:C_MUR + 8], -1.0, 1.0,
            ALU.mult, ALU.add), waits=[tok_consts], inc="V")
        E("vector", lambda e: e.tensor_scalar(
            CONST[:, C_OMK:C_OMK + 8], CONST[:, C_MUK:C_MUK + 8], -1.0, 1.0,
            ALU.mult, ALU.add), inc="V")
        E("vector", lambda e: e.memset(CB[:, CB_UCOL:CB_UCOL + 1], 1.0 / S), inc="V")
        t_eps = E("vector", lambda e: e.memset(CONST[:, C_EPS:C_EPS + 1], 1e-5), inc="V")
        t_ones1 = E("vector", lambda e: e.memset(CB[0:1, CB_ONES1:CB_ONES1 + 128], 1.0), inc="V")
        t_bvrowbf = E("vector", lambda e: e.tensor_copy(
            CB[0:1, CB_BVROW:CB_BVROW + 1024], POOL[0:1, 0:1024]),
            waits=[tok_consts], inc="V")

        ps_free = {"psA": [], "psB": []}
        pool_state = [[t_bvrowbf], [], []]
        prev_attn_P_done = None
        prev_mix_done = None
        yreads_done = []
        prev_final_reads = []
        s5x_state = [[] for _ in range(NSL)]
        chain_state = {"X": []}

        def ln_chain(cset, lo, width, stat_toks, sum_ap=None):
            cs = CONST
            if cset == "A":
                c_sum, c_sq, c_rstd, c_nm = C_SUM + lo, C_SQ + lo, C_RSTD + lo, C_NM + lo
                c_mean, c_var, c_msq = XA_MEAN, XA_VAR, XA_MSQ
            else:
                c_sum, c_sq, c_rstd, c_nm = B_SUM + lo, B_SQ + lo, B_RSTD + lo, B_NM + lo
                c_mean, c_var, c_msq = XB_MEAN + lo, XB_VAR + lo, XB_MSQ + lo
            sl = lambda c: cs[:, c:c + width]
            s_ap = sl(c_sum) if sum_ap is None else sum_ap
            E("vector", lambda e: e.tensor_scalar(
                sl(c_mean), s_ap, 1.0 / D, None, ALU.mult),
                waits=list(stat_toks) + chain_state["X"], inc="V")
            E("vector", lambda e: e.tensor_tensor(sl(c_msq), sl(c_mean), sl(c_mean), ALU.mult), inc="V")
            tv = E("vector", lambda e: e.scalar_tensor_tensor(
                sl(c_var), sl(c_sq), 1.0 / D, sl(c_msq), ALU.mult, ALU.subtract), inc="V")
            E("scalar", lambda e: e.activation(sl(c_var), sl(c_var), AF.Ln,
                                               bias=CONST[:, C_EPS:C_EPS + 1]),
              waits=[tv, t_eps], inc="A")
            ta = E("scalar", lambda e: e.activation(sl(c_rstd), sl(c_var), AF.Exp, scale=-0.5),
                   inc="A")
            tn = E("vector", lambda e: e.tensor_scalar(
                sl(c_nm), sl(c_mean), -1.0, None, ALU.mult), inc="V")
            chain_state["X"] = [ta, tn]
            return ta, tn

        def stat_pair(src_ap, sum_col, sq_col, dumpA, dumpV, ready, extra=()):
            tvs = E("scalar", lambda e: e.activation(
                dumpV, src_ap, AF.Identity,
                accum_out=CONST[:, sum_col:sum_col + 1]),
                waits=[ready] + list(extra), inc="A")
            tas = E("scalar", lambda e: e.activation(
                dumpA, src_ap, AF.Square,
                accum_out=CONST[:, sq_col:sq_col + 1]),
                waits=[ready] + list(extra), inc="A")
            return tvs, tas

        for n in range(nseq):
            if dbg == f"end@{n - 1}":
                break
            ntile_state = [[], []]
            evac_toks = []
            p1_pstate = {0: ps_free["psA"], 1: ps_free["psA"],
                         2: ps_free["psB"], 3: ps_free["psB"]}
            for t0 in range(0, TC, 2):
                stat_toks = []
                xt = {}
                for tc in (t0, t0 + 1):
                    slot = tc % NSL
                    src = S5[:, 2048 * slot:2048 * slot + 2048].bitcast(F32)
                    tok = E("sync", lambda e, tc=tc, slot=slot, n=n, src=src: e.dma_start(
                        src, xd[n, 128 * tc:128 * tc + 128, :]),
                        waits=s5x_state[slot],
                        inc=("dXa", "dXb", "dXc", "dXd", "dXe", "dXf", "dXg", "dXh")[slot], amt=16)
                    xt[tc] = (src, slot)
                    tvs, tas = stat_pair(src, C_SUM + tc, C_SQ + tc, S2[:, 0:1024], S2[:, 1024:2048], tok)
                    stat_toks += [tvs, tas]
                ta_r, _ = ln_chain("A", t0, 2, stat_toks)
                for tc in (t0, t0 + 1):
                    src, slot = xt[tc]
                    nt = S3[:, NT_LO + 1024 * (tc % 2):NT_LO + 1024 * (tc % 2) + 1024]
                    mixw = ([prev_mix_done] + prev_final_reads) if (n > 0 and tc <= 1) else []
                    ta = E("vector", lambda e, src=src, nt=nt, tc=tc: e.tensor_scalar(
                        nt, src,
                        CONST[:, C_NM + tc:C_NM + tc + 1],
                        CONST[:, C_RSTD + tc:C_RSTD + tc + 1],
                        ALU.add, ALU.mult),
                        waits=[ta_r] + ntile_state[tc % 2] + mixw, inc="V")
                    s5x_state[slot] = [ta]
                    si = tc % 4
                    tp = None
                    for cj in range(CC):
                        tp = E("tensor", lambda e, si=si, nt=nt, cj=cj: e.matmul(
                            pair_ap(si)[:, 128 * cj:128 * cj + 128],
                            nt[:, 128 * cj:128 * cj + 128],
                            CB[:, CB_I:CB_I + 128], start=True, stop=True),
                            waits=([ta] + p1_pstate[si]) if cj == 0 else (),
                            inc="P" if cj == CC - 1 else None)
                    ntile_state[tc % 2] = [tp]
                    tv = E("vector", lambda e, si=si, tc=tc: e.tensor_copy(
                        bass.AP(S1, 128 * tc, [[FS, 128], [S, CC], [1, 128]]),
                        pair_ap(si)[:, 0:1024]),
                        waits=[tp], inc="V")
                    p1_pstate[si] = [tv]
                    evac_toks.append(tv)
            xn1_tok = evac_toks[-1]
            ps_free["psA"] = p1_pstate[0] + p1_pstate[1]
            ps_free["psB"] = p1_pstate[2] + p1_pstate[3]

            if dbg == f"xn1@{n}":
                E("sync", lambda e: e.dma_start(dbg1[:, :], S1[:, :]),
                  waits=[xn1_tok], inc="dO", amt=16)
                break
            def proj(Wt, src, evac_fn, first_waits, cj_toks=None):
                NHB = 2 if NQB >= 2 else 1
                HQB = NQB // NHB
                HW = 512 * HQB
                evac = {}
                tmm = None
                for dj in range(CC):
                    base = psA if dj % 2 == 0 else psB
                    for hb in range(NHB):
                        ps = base[:, HW * hb:HW * hb + HW]
                        if dj < 2:
                            w = list(first_waits) + ps_free["psA" if dj % 2 == 0 else "psB"]
                        else:
                            w = [evac[(dj - 2, hb)]]
                        for cj in range(CC):
                            for qb in range(HQB):
                                last = (cj == CC - 1 and qb == HQB - 1)
                                waits = w if (cj == 0 and qb == 0) else ()
                                if cj_toks is not None and dj == 0 and qb == 0:
                                    waits = list(waits) + [cj_toks[cj]]
                                tmm = E("tensor", lambda e, ps=ps, dj=dj, cj=cj, qb=qb, hb=hb, HW=HW, Wt=Wt, src=src: e.matmul(
                                    ps[:, 512 * qb:512 * qb + 512],
                                    Wt[:, 1024 * cj + 128 * dj:1024 * cj + 128 * dj + 128],
                                    src[:, S * cj + HW * hb + 512 * qb:S * cj + HW * hb + 512 * qb + 512],
                                    start=(cj == 0), stop=(cj == CC - 1)),
                                    waits=waits,
                                    inc="P" if last else None)
                        evac[(dj, hb)] = evac_fn(dj, hb, HW, ps, tmm)
                ps_free["psA"] = [evac[(CC - 2, NHB - 1)]]
                ps_free["psB"] = [evac[(CC - 1, NHB - 1)]]
                return tmm, {dj: evac[(dj, NHB - 1)] for dj in range(CC)}

            def evac_qk(Wdst, bcol):
                def f(dj, hb, HW, ps, ptok):
                    return E("scalar", lambda e, dj=dj, hb=hb, HW=HW, ps=ps: e.activation(
                        Wdst[:, S * dj + HW * hb:S * dj + HW * hb + HW], ps[:, 0:HW],
                        AF.Identity,
                        bias=CONST[:, bcol + dj:bcol + dj + 1]),
                        waits=[ptok, tok_consts] + prev_final_reads, inc="A")
                return f

            t_ones = E("vector", lambda e: e.memset(
                bass.AP(S4, 64, [[TC * 1040, 128], [1040, TC], [65, NH], [1, 1]]), 1.0),
                waits=[prev_attn_P_done], inc="V")

            vpair_state = {0: ps_free["psA"], 1: ps_free["psA"],
                           2: ps_free["psB"], 3: ps_free["psB"]}
            t_vev = None
            tv_mm = None
            for tc in range(TC):
                si = tc % 4
                pst = psA if si < 2 else psB
                poff = 1024 * (si % 2)
                for hf in range(2):
                    E("tensor", lambda e, si=si, hf=hf: e.matmul(
                        pair_ap(si)[:, 512 * hf:512 * hf + 512],
                        CB[0:1, CB_ONES1:CB_ONES1 + 128],
                        CB[0:1, CB_BVROW + 512 * hf:CB_BVROW + 512 * hf + 512],
                        start=True, stop=False),
                        waits=([t_ones1, t_bvrowbf] + vpair_state[si]) if hf == 0 else ())
                for cj in range(CC):
                    for hf in range(2):
                        last = (cj == CC - 1 and hf == 1)
                        tv_mm = E("tensor", lambda e, si=si, tc=tc, cj=cj, hf=hf: e.matmul(
                            pair_ap(si)[:, 512 * hf:512 * hf + 512],
                            S1[:, S * cj + 128 * tc:S * cj + 128 * tc + 128],
                            W0[:, 1024 * cj + 512 * hf:1024 * cj + 512 * hf + 512],
                            start=False, stop=(cj == CC - 1)),
                            waits=[t_w0pre, evac_toks[tc]] if (cj == 0 and hf == 0) else (),
                            inc="P" if last else None)
                t_vev = E("vector", lambda e, si=si, tc=tc, pst=pst, poff=poff: e.tensor_copy(
                    bass.AP(S4, 1040 * tc, [[TC * 1040, 128], [65, NH], [1, 64]]),
                    bass.AP(pst, poff, [[2048, 128], [64, NH], [1, 64]])),
                    waits=[tv_mm, t_ones], inc="V")
                vpair_state[si] = [t_vev]
            V_tok = t_vev
            ps_free["psA"] = vpair_state[0] + vpair_state[1]
            ps_free["psB"] = vpair_state[2] + vpair_state[3]

            t_wqa = E("gpsimd", lambda e: e.dma_start(
                W0[:, 0:4096], bass.AP(wq, 0, [[1024, 128], [131072, 4], [1, 1024]])),
                waits=[tv_mm], inc="dW0", amt=16)
            t_wqb = E("gpsimd", lambda e: e.dma_start(
                W0[:, 4096:8192], bass.AP(wq, 4 * 131072, [[1024, 128], [131072, 4], [1, 1024]])),
                waits=[tv_mm], inc="dW2", amt=16)

            tk_mm, tk_ev = proj(W1, S1, evac_qk(S3, C_BK), [xn1_tok, t_w1pre])
            KT_tok = tk_ev[CC - 1]
            t_wr = E("gpsimd", lambda e: e.dma_start(
                W1[:, :], bass.AP(wr, 0, [[1024, 128], [131072, 8], [1, 1024]])),
                waits=[tk_mm], inc="dW1", amt=16)

            tq_mm, tq_ev = proj(W0, S1, evac_qk(S2, C_BQ), [t_wqa, t_wqb])
            QT_tok = tq_ev[CC - 1]
            t_wkf = E("gpsimd", lambda e: e.dma_start(
                W0[:, :], bass.AP(wkf, 0, [[1024, 128], [131072, 8], [1, 1024]])),
                waits=[tq_mm], inc="dW0", amt=16)

            if dbg == f"qkv@{n}":
                E("sync", lambda e: e.dma_start(dbg1[:, :], S2[:, :]), waits=[QT_tok], inc="dO", amt=16)
                E("sync", lambda e: e.dma_start(dbg2[:, :], S3[:, :]), waits=[KT_tok], inc="dO", amt=16)
                E("sync", lambda e: e.dma_start(dbg3[:, :], S4[:, :]), waits=[V_tok], inc="dO", amt=16)
                break
            QW = min(1024, S)
            NQH = S // QW
            sc_state = {i: list(ps_free["psB"]) for i in range(4)}
            psA_war = list(ps_free["psA"]) + [V_tok]
            pt_state = [[] for _ in range(NPT)]
            osb_war = []
            attn_last = []
            sc_i = 0
            pt_i = 0
            last_P3_P = None
            for hp in range(NH // 2):
                ch = hp
                for qh in range(NQH):
                    qlo = QW * qh
                    qhi = qlo + QW
                    blocks = []
                    pre_us = []
                    subs_seq = []
                    for ki in range(KI):
                        ncols = 128 * (ki + 1)
                        dlo = ncols - 128
                        lo, hi = qlo, min(qhi, ncols)
                        if hi <= lo:
                            if qh == NQH - 1:
                                pre_us.append(ki)
                                subs_seq.append(("u", ki))
                            continue
                        for c0 in range(lo, hi, 512):
                            c1 = min(c0 + 512, hi)
                            subs = []
                            for cs in (c0,):
                                ce = c1
                                if cs < dlo < ce:
                                    subs.append((cs, dlo))
                                    subs.append((dlo, ce))
                                else:
                                    subs.append((cs, ce))
                            if ki == KI - 1:
                                subs = [(a, min(bb, S - 1)) for a, bb in subs if a < S - 1]
                            blk = {"ki": ki, "c0": c0, "c1": c1, "subs": subs,
                                   "diag": c0 <= dlo < c1,
                                   "u": (qh == NQH - 1 and c1 == hi)}
                            blocks.append(blk)
                            for sub in subs:
                                subs_seq.append(("pv", sub))
                            if blk["u"]:
                                subs_seq.append(("u", ki))
                    subs_seq = ([("u", ki) for ki in pre_us]
                                + [sq for sq in subs_seq if not (sq[0] == "u" and sq[1] in pre_us)])
                    first_of_bank = {}
                    last_of_bank = {}
                    for i, sq in enumerate(subs_seq):
                        bk = ((S - 1 - qlo) // 512) if sq[0] == "u" else ((sq[1][0] - qlo) // 512)
                        first_of_bank.setdefault(bk, i)
                        last_of_bank[bk] = i

                    def emit_S_pair(blk, ch=ch):
                        ki, c0, c1 = blk["ki"], blk["c0"], blk["c1"]
                        for e2 in (0, 1):
                            slot = blk["slotA"] + e2
                            rb = 64 * e2
                            waits = []
                            if e2 == 0:
                                waits = (sc_state[slot] + sc_state[slot + 1]
                                         + ([QT_tok, KT_tok] if sc_i <= 2 else []))
                            blk["tS"] = E("tensor", lambda e, slot=slot, ki=ki, c0=c0, c1=c1, ch=ch, rb=rb: e.matmul(
                                psB[:, 512 * slot:512 * slot + (c1 - c0)],
                                S3[rb:rb + 64, S * ch + 128 * ki:S * ch + 128 * ki + 128],
                                S2[rb:rb + 64, S * ch + c0:S * ch + c1],
                                start=True, stop=True),
                                waits=waits,
                                inc="P" if e2 == 1 else None)

                    def emit_E_pair(blk):
                        slotA, half = blk["slotA"], blk["half"]
                        w = blk["c1"] - blk["c0"]
                        if w == 512:
                            t_exp = E("scalar", lambda e, slotA=slotA, half=half: e.activation(
                                PT(half)[:, 0:1024], psB[:, 512 * slotA:512 * slotA + 1024],
                                AF.Exp, scale=0.25),
                                waits=[blk["tS"]] + pt_state[half], inc="A")
                        else:
                            E("scalar", lambda e, slotA=slotA, half=half, w=w: e.activation(
                                PT(half)[:, 0:w], psB[:, 512 * slotA:512 * slotA + w],
                                AF.Exp, scale=0.25),
                                waits=[blk["tS"]] + pt_state[half], inc="A")
                            t_exp = E("scalar", lambda e, slotA=slotA, half=half, w=w: e.activation(
                                PT(half)[:, 512:512 + w],
                                psB[:, 512 * slotA + 512:512 * slotA + 512 + w],
                                AF.Exp, scale=0.25), inc="A")
                        sc_state[slotA] = [t_exp]
                        sc_state[slotA + 1] = [t_exp]
                        blk["te"] = t_exp
                        if blk["diag"]:
                            c0 = blk["c0"]
                            dlo = 128 * (blk["ki"] + 1) - 128
                            tm = None
                            for e2 in (0, 1):
                                tm = E("vector", lambda e, half=half, c0=c0, dlo=dlo, e2=e2: e.tensor_tensor(
                                    PT(half)[:, 512 * e2 + dlo - c0:512 * e2 + dlo - c0 + 128],
                                    PT(half)[:, 512 * e2 + dlo - c0:512 * e2 + dlo - c0 + 128],
                                    CB[:, CB_MASK:CB_MASK + 128], ALU.mult),
                                    waits=[t_exp], inc="V" if e2 == 1 else None)
                            blk["tm"] = tm

                    def emit_PV_pair(blk, hp=hp, qlo=qlo):
                        ki, c0, half = blk["ki"], blk["c0"], blk["half"]
                        dlo = 128 * ki
                        last_pv = None
                        n_subs = len(blk["subs"]) + (1 if blk["u"] else 0)
                        for e2 in (0, 1):
                            h = 2 * hp + e2
                            si = pv_ctx["sub_i"]
                            for (cs, ce) in blk["subs"]:
                                bk = (cs - qlo) // 512
                                fb = first_of_bank[bk] == si
                                lb = last_of_bank[bk] == si
                                waits = [blk["te"]]
                                if cs <= dlo < ce:
                                    waits.append(blk.get("tm"))
                                if si == 0 and e2 == 0:
                                    waits += psA_war
                                last_pv = E("tensor", lambda e, ki=ki, h=h, e2=e2, cs=cs, ce=ce, half=half, c0=c0, fb=fb, lb=lb, qlo=qlo: e.matmul(
                                    psA[0:65, QW * e2 + (cs - qlo):QW * e2 + (ce - qlo)],
                                    S4[:, 1040 * ki + 65 * h:1040 * ki + 65 * h + 65],
                                    PT(half)[:, 512 * e2 + (cs - c0):512 * e2 + (ce - c0)],
                                    start=fb, stop=lb),
                                    waits=waits, inc="P")
                                si += 1
                            if blk["u"]:
                                bk = (S - 1 - qlo) // 512
                                fb = first_of_bank[bk] == si
                                lb = last_of_bank[bk] == si
                                last_pv = E("tensor", lambda e, ki=ki, h=h, e2=e2, fb=fb, lb=lb, qlo=qlo: e.matmul(
                                    psA[0:65, QW * e2 + (S - 1 - qlo):QW * e2 + (S - qlo)],
                                    S4[:, 1040 * ki + 65 * h:1040 * ki + 65 * h + 65],
                                    CB[:, CB_UCOL:CB_UCOL + 1],
                                    start=fb, stop=lb),
                                    waits=(), inc="P")
                                si += 1
                        pv_ctx["sub_i"] = si
                        pt_state[half] = [last_pv]
                        pv_ctx["last_pv"] = last_pv

                    pv_ctx = {"last_pv": None, "sub_i": 0}
                    for ui, ki in enumerate(pre_us):
                        si = pv_ctx["sub_i"]
                        bk = (S - 1 - qlo) // 512
                        fb = first_of_bank[bk] == si
                        lb = last_of_bank[bk] == si
                        for e2 in (0, 1):
                            h = 2 * hp + e2
                            waits = psA_war if (si == 0 and e2 == 0) else []
                            pv_ctx["last_pv"] = E("tensor", lambda e, ki=ki, h=h, e2=e2, fb=fb, lb=lb, qlo=qlo: e.matmul(
                                psA[0:65, QW * e2 + (S - 1 - qlo):QW * e2 + (S - qlo)],
                                S4[:, 1040 * ki + 65 * h:1040 * ki + 65 * h + 65],
                                CB[:, CB_UCOL:CB_UCOL + 1],
                                start=fb, stop=lb),
                                waits=waits, inc="P")
                        pv_ctx["sub_i"] = si + 1
                    pending = None
                    for blk in blocks:
                        blk["slotA"] = (sc_i % 2) * 2
                        sc_i += 1
                        blk["half"] = pt_i % NPT
                        pt_i += 1
                        emit_S_pair(blk)
                        if pending is not None:
                            emit_PV_pair(pending)
                        emit_E_pair(blk)
                        pending = blk
                    emit_PV_pair(pending)
                    last_pv = pv_ctx["last_pv"]

                    t_osb = E("vector", lambda e: e.tensor_copy(
                        S1[0:65, OSB_LO:OSB_LO + 2 * QW], psA[0:65, 0:2 * QW]),
                        waits=[last_pv] + osb_war, inc="V")
                    psA_war = [t_osb]
                    for e2 in (0, 1):
                        h = 2 * hp + e2
                        for g0 in range(0, QW // 128, 4):
                            qcs = [QW // 128 * qh + g0 + j for j in range(4)]
                            slot = sc_i % 4
                            sc_i += 1
                            tT = None
                            for j, qc in enumerate(qcs):
                                jl = g0 + j
                                tT = E("tensor", lambda e, slot=slot, e2=e2, jl=jl, j=j: e.matmul(
                                    psB[:, 512 * slot + 128 * j:512 * slot + 128 * j + 65],
                                    S1[0:65, OSB_LO + QW * e2 + 128 * jl:OSB_LO + QW * e2 + 128 * jl + 128],
                                    CB[0:65, CB_I:CB_I + 65], start=True, stop=True),
                                    waits=([t_osb] + sc_state[slot]) if j == 0 else (),
                                    inc="P" if j == len(qcs) - 1 else None)
                            osb_war = [tT]
                            last_P3_P = tT
                            t_rcp = E("vector", lambda e, slot=slot, nq=len(qcs): e.reciprocal(
                                CONST[:, C_RCP + 4 * slot:C_RCP + 4 * slot + nq],
                                bass.AP(psB, 512 * slot + 64, [[2048, 128], [128, nq], [1, 1]])),
                                waits=[tT], inc="V")
                            tnv = None
                            for j, qc in enumerate(qcs):
                                tnv = E("vector", lambda e, slot=slot, qc=qc, j=j, h=h: e.tensor_scalar(
                                    S5[:, 1024 * qc + 64 * h:1024 * qc + 64 * h + 64],
                                    psB[:, 512 * slot + 128 * j:512 * slot + 128 * j + 64],
                                    CONST[:, C_RCP + 4 * slot + j:C_RCP + 4 * slot + j + 1],
                                    None, ALU.mult),
                                    inc="V" if j == len(qcs) - 1 else None)
                            sc_state[slot] = [tnv]
                            attn_last = [tnv]
            attn_done = attn_last
            prev_attn_P_done = last_P3_P
            ps_free["psA"] = psA_war
            ps_free["psB"] = sc_state[0] + sc_state[1]

            if dbg == f"attn@{n}":
                E("sync", lambda e: e.dma_start(dbg1[:, :], S5[:, :]),
                  waits=attn_done, inc="dO", amt=16)
                break
            z_evac_toks = []
            p4_pstate = {0: list(ps_free["psA"]), 1: list(ps_free["psA"]),
                         2: list(ps_free["psB"]), 3: list(ps_free["psB"])}
            ntile_state = [[], []]
            y_state = {0: list(pool_state[2]), 1: []}
            Yap = {0: POOL[:, 2048:3072], 1: S2[:, 2048:4096].bitcast(F32)}
            p4_attn_reads = []
            tysum = E("vector", lambda e: e.tensor_copy(
                SCR[:, 0:TC], CONST[:, C_SUM:C_SUM + TC]),
                waits=attn_done, inc="V")
            NXS = 4 if FS >= 10240 else 2
            XS = {0: POOL[:, 0:1024], 1: POOL[:, 1024:2048]}
            XSTRM = {0: "dX0", 1: "dX1", 2: "dXe", 3: "dXf"}
            if NXS == 4:
                XS[2] = S3[:, 4096:6144].bitcast(F32)
                XS[3] = S3[:, 6144:8192].bitcast(F32)
            xslot_state = {0: list(pool_state[0]), 1: list(pool_state[1]),
                           2: [prev_attn_P_done, ("dXg", b.cnt["dXg"])],
                           3: [prev_attn_P_done, ("dXh", b.cnt["dXh"])]}
            xtoks = {}

            def issue_x(tc):
                slot = tc % NXS
                xtoks[tc] = E("sync", lambda e, tc=tc, slot=slot, n=n: e.dma_start(
                    XS[slot], xd[n, 128 * tc:128 * tc + 128, :]),
                    waits=xslot_state[slot], inc=XSTRM[slot], amt=16)

            if NXS == 4:
                issue_x(0)
                issue_x(1)
            for t0 in range(0, TC, 2):
                if NXS == 4:
                    if t0 + 2 < TC:
                        issue_x(t0 + 2)
                        issue_x(t0 + 3)
                else:
                    issue_x(t0)
                    issue_x(t0 + 1)
                stat_toks = []
                for tc in (t0, t0 + 1):
                    for tok in stat_pair(S5[:, 1024 * tc:1024 * tc + 1024], C_SUM + tc, C_SQ + tc,
                                         S2[:, 0:1024], S2[:, 1024:2048],
                                         attn_done[0],
                                         extra=list(attn_done[1:]) + [tysum]):
                        stat_toks.append(tok)
                        p4_attn_reads.append(tok)
                ta_r, _ = ln_chain("A", t0, 2, stat_toks)
                ystat_toks = []
                yinfo = {}
                for tc in (t0, t0 + 1):
                    p = tc % 2
                    Y = Yap[p]
                    t1 = E("vector", lambda e, tc=tc, Y=Y: e.tensor_scalar(
                        Y, S5[:, 1024 * tc:1024 * tc + 1024],
                        CONST[:, C_NM + tc:C_NM + tc + 1],
                        CONST[:, C_RSTD + tc:C_RSTD + tc + 1],
                        ALU.add, ALU.mult),
                        waits=[ta_r] + y_state[p], inc="V")
                    p4_attn_reads.append(t1)
                    t2 = E("vector", lambda e, Y=Y, XT=XS[tc % NXS]: e.tensor_tensor(
                        Y, Y, XT, ALU.add),
                        waits=[xtoks[tc]], inc="V")
                    xslot_state[tc % NXS] = [t2]
                    if tc % NXS in (0, 1):
                        pool_state[tc % NXS] = [t2]
                    tydma = E("sync", lambda e, tc=tc, Y=Y: e.dma_start(
                        yscr[128 * tc:128 * tc + 128, :], Y),
                        waits=[t2] + yreads_done, inc="dY%d" % (tc % 4), amt=16)
                    tas = E("scalar", lambda e, Y=Y, p=p: e.activation(
                        S3[:, 0:1024], Y, AF.Square,
                        accum_out=CONST[:, B_SQ + p:B_SQ + p + 1]),
                        waits=[t2], inc="A")
                    ystat_toks.append(tas)
                    yinfo[tc] = (Y, tydma, tas)
                tb_r, _ = ln_chain("B", 0, 2, [tysum] + ystat_toks,
                                   sum_ap=SCR[:, t0:t0 + 2])
                for tc in (t0, t0 + 1):
                    p = tc % 2
                    Y, tydma, tas = yinfo[tc]
                    nt = S3[:, NT_LO + 1024 * p:NT_LO + 1024 * p + 1024]
                    tz = E("vector", lambda e, Y=Y, nt=nt, p=p: e.tensor_scalar(
                        nt, Y,
                        CONST[:, B_NM + p:B_NM + p + 1],
                        CONST[:, B_RSTD + p:B_RSTD + p + 1],
                        ALU.add, ALU.mult),
                        waits=[tb_r] + ntile_state[p], inc="V")
                    y_state[p] = [tz, tydma, tas]
                    si = tc % 4
                    tp = None
                    for cj in range(CC):
                        tp = E("tensor", lambda e, si=si, nt=nt, cj=cj: e.matmul(
                            pair_ap(si)[:, 128 * cj:128 * cj + 128],
                            nt[:, 128 * cj:128 * cj + 128],
                            CB[:, CB_I:CB_I + 128], start=True, stop=True),
                            waits=([tz] + p4_pstate[si]) if cj == 0 else (),
                            inc="P" if cj == CC - 1 else None)
                    ntile_state[p] = [tp]
                    tvz = E("vector", lambda e, si=si, tc=tc: e.tensor_copy(
                        bass.AP(S1, 128 * tc, [[FS, 128], [S, CC], [1, 128]]),
                        pair_ap(si)[:, 0:1024]),
                        waits=[tp], inc="V")
                    p4_pstate[si] = [tvz]
                    z_evac_toks.append(tvz)
            zT_tok = z_evac_toks[-1]
            ywrites_done = [("dY%d" % i, b.cnt["dY%d" % i]) for i in range(4)]
            pool_state[2] = list(y_state[0]) + list(y_state[1])
            ps_free["psA"] = p4_pstate[0] + p4_pstate[1]
            ps_free["psB"] = p4_pstate[2] + p4_pstate[3]

            if dbg == f"z@{n}":
                E("sync", lambda e: e.dma_start(dbg1[:, :], S1[:, :]),
                  waits=[zT_tok], inc="dO", amt=16)
                break
            if dbg == f"ffn@{n}":
                pass
            dst_toks = {}
            xr_cc_toks = []
            tmp_state = [[], []]
            ci = 0
            for (dst, mucol, omcol) in ((S2, C_MUR, C_OMR), (S3, C_MUK, C_OMK)):
                tmix = None
                for cc in range(CC):
                    lo = S * cc
                    tb = ci % 2
                    ci += 1
                    tmp = POOL[:, 1024 * tb:1024 * tb + 1024].bitcast(BF16)
                    w = list(tmp_state[tb])
                    if cc == 0 and dst is S2:
                        w += ([zT_tok, prev_attn_P_done] + pool_state[0]
                              + pool_state[1] + pool_state[2] + ywrites_done)
                    if dst is S2:
                        ttmp = E("scalar", lambda e, lo=lo, omcol=omcol, cc=cc, tmp=tmp: e.activation(
                            tmp[:, 0:S - 1], S1[:, lo:lo + S - 1], AF.Identity,
                            scale=CONST[:, omcol + cc:omcol + cc + 1]),
                            waits=w, inc="A")
                    else:
                        ttmp = E("vector", lambda e, lo=lo, mucol=mucol, cc=cc, tmp=tmp: e.tensor_scalar(
                            tmp[:, 0:S - 1], S1[:, lo:lo + S - 1],
                            CONST[:, omcol + cc:omcol + cc + 1], None, ALU.mult),
                            waits=w, inc="V")
                    tst = E("vector", lambda e, lo=lo, dst=dst, mucol=mucol, cc=cc, tmp=tmp: e.scalar_tensor_tensor(
                        dst[:, lo + 1:lo + S], S1[:, lo + 1:lo + S],
                        CONST[:, mucol + cc:mucol + cc + 1],
                        tmp[:, 0:S - 1], ALU.mult, ALU.add),
                        waits=[ttmp], inc="V")
                    tmix = E("vector", lambda e, lo=lo, dst=dst, mucol=mucol, cc=cc: e.tensor_scalar(
                        dst[:, lo:lo + 1], S1[:, lo:lo + 1],
                        CONST[:, mucol + cc:mucol + cc + 1], None, ALU.mult),
                        inc="V")
                    tmp_state[tb] = [tst]
                    if dst is S2:
                        xr_cc_toks.append(tmix)
                dst_toks[id(dst)] = tmix
            xr_tok = dst_toks[id(S2)]
            xk_tok = dst_toks[id(S3)]
            mix_tok = xk_tok
            prev_mix_done = mix_tok
            pool_state[0] = [xk_tok]

            def evac_r(dj, hb, HW, ps, ptok):
                return E("scalar", lambda e, dj=dj, hb=hb, HW=HW, ps=ps: e.activation(
                    S4[:, S * dj + HW * hb:S * dj + HW * hb + HW], ps[:, 0:HW],
                    AF.Tanh, scale=0.5),
                    waits=[ptok, V_tok], inc="A")

            tr_mm, _ = proj(W1, S2, evac_r, [xr_cc_toks[0], t_wr], cj_toks=xr_cc_toks)
            t_wvf = E("gpsimd", lambda e: e.dma_start(
                W1[:, :], bass.AP(wvf, 0, [[1024, 128], [131072, 8], [1, 1024]])),
                waits=[tr_mm], inc="dW1", amt=16)

            krelu_state = {}

            def evac_k(dj, hb, HW, ps, ptok):
                kb = dj % 2
                krelu = POOL[:, 1024 * kb:1024 * kb + 1024].bitcast(BF16)
                kr = krelu[:, HW * hb:HW * hb + HW]
                tkr = E("vector", lambda e, ps=ps, kr=kr, HW=HW: e.tensor_scalar(
                    kr, ps[:, 0:HW], 0.0, 0.70710678118654755,
                    ALU.max, ALU.mult),
                    waits=[ptok] + krelu_state.get((kb, hb), [])
                    + (p4_attn_reads[-3:] if dj < 2 else []),
                    inc="V")
                tsq = E("scalar", lambda e, dj=dj, hb=hb, HW=HW, kr=kr: e.activation(
                    S5[:, S * dj + HW * hb:S * dj + HW * hb + HW], kr, AF.Square),
                    waits=[tkr], inc="A")
                krelu_state[(kb, hb)] = [tsq]
                return tkr

            tk2_mm, _ = proj(W0, S3, evac_k, [xk_tok, t_wkf])
            kact_done = [t for v in krelu_state.values() for t in v]
            pool_state[0] = [t for (kb, hb), v in krelu_state.items() if kb == 0 for t in v]
            pool_state[1] = [t for (kb, hb), v in krelu_state.items() if kb == 1 for t in v]

            ft_state = [[], []]
            ffn_ev = {}
            vp_last = None
            for dj in range(CC):
                ps = psA if dj % 2 == 0 else psB
                if dj < 2:
                    w = [t_wvf] + kact_done + ps_free["psA" if dj % 2 == 0 else "psB"]
                else:
                    w = [ffn_ev[dj - 2]]
                tmm = None
                for cj in range(CC):
                    for qb in range(NQB):
                        last = (cj == CC - 1 and qb == NQB - 1)
                        tmm = E("tensor", lambda e, ps=ps, dj=dj, cj=cj, qb=qb: e.matmul(
                            ps[:, 512 * qb:512 * qb + 512],
                            W1[:, 1024 * cj + 128 * dj:1024 * cj + 128 * dj + 128],
                            S5[:, S * cj + 512 * qb:S * cj + 512 * qb + 512],
                            start=(cj == 0), stop=(cj == CC - 1)),
                            waits=w if (cj == 0 and qb == 0) else (),
                            inc="P" if last else None)
                vp_last = tmm
                ft = dj % 2
                tmul = E("vector", lambda e, dj=dj, ps=ps, ft=ft: e.scalar_tensor_tensor(
                    W0[:, 2048 * ft:2048 * ft + S], S4[:, S * dj:S * dj + S], 1.0,
                    ps[:, 0:S], ALU.add, ALU.mult),
                    waits=[tmm] + ft_state[ft], inc="V")
                tT = None
                evs = []
                for g, tcs in enumerate(chunks8(TC)):
                    for j, tc in enumerate(tcs):
                        tT = E("tensor", lambda e, ps=ps, ft=ft, tc=tc, g=g, j=j: e.matmul(
                            ps[:, 1024 * g + 128 * j:1024 * g + 128 * j + 128],
                            W0[:, 2048 * ft + 128 * tc:2048 * ft + 128 * tc + 128],
                            CB[:, CB_I:CB_I + 128], start=True, stop=True),
                            waits=[tmul] if (g == 0 and j == 0) else (),
                            inc="P" if j == len(tcs) - 1 else None)
                    tev = E("vector", lambda e, ps=ps, dj=dj, g=g, ntc=len(tcs): e.tensor_copy(
                        bass.AP(S2, 1024 * 8 * g + 128 * dj,
                                [[FS, 128], [1024, ntc], [1, 128]]),
                        ps[:, 1024 * g:1024 * g + 128 * ntc]),
                        waits=[tT], inc="V")
                    evs.append(tev)
                ft_state[ft] = [tT]
                ffn_ev[dj] = evs[-1]
            ps_free["psA"] = [ffn_ev[CC - 2]]
            ps_free["psB"] = [ffn_ev[CC - 1]]
            ffn_done = list(ffn_ev.values())

            if n + 1 < nseq and dbg != "endnp@0":
                t_w0pre = E("gpsimd", lambda e: e.dma_start(
                    W0[:, :], bass.AP(wv, 0, [[1024, 128], [131072, 8], [1, 1024]])),
                    waits=ft_state[0] + ft_state[1], inc="dW0", amt=16)
                t_w1pre = E("gpsimd", lambda e: e.dma_start(
                    W1[:, :], bass.AP(wk, 0, [[1024, 128], [131072, 8], [1, 1024]])),
                    waits=[vp_last], inc="dW1", amt=16)

            if dbg == f"ffn@{n}":
                E("sync", lambda e: e.dma_start(dbg1[:, :], S2[:, :]),
                  waits=ffn_done, inc="dO", amt=16)
                E("sync", lambda e: e.dma_start(dbg2[:, :], S5[:, :]),
                  waits=kact_done, inc="dO", amt=16)
                E("sync", lambda e: e.dma_start(dbg3[:, 0:8 * S // 2], S4[:, 0:8 * S // 2].bitcast(BF16)[:, 0:8 * S // 2] if False else S4[:, 0:8 * S // 2]),
                  waits=ffn_done, inc="dO", amt=16)
                break
            NOB = 4 if FS >= 8192 else 2
            o_state = {k: [] for k in range(NOB)}
            o_state[0] = list(pool_state[2])
            Oap = {0: POOL[:, 2048:3072], 1: S3[:, 2048:4096].bitcast(F32)}
            if NOB == 4:
                Oap[2] = S1[:, 0:2048].bitcast(F32)
                Oap[3] = S1[:, 2048:4096].bitcast(F32)
            YSTRM = {0: "dYr0", 1: "dYr1", 2: "dXg", 3: "dXh"}
            yslot_state = {0: list(pool_state[0]), 1: list(pool_state[1]),
                           2: [tk2_mm, ("dXe", b.cnt["dXe"])] + xslot_state[2],
                           3: [tk2_mm, ("dXf", b.cnt["dXf"])] + xslot_state[3]}
            ytoks = {}

            def issue_y(tc):
                slot = tc % NXS
                ytoks[tc] = E("sync", lambda e, tc=tc, slot=slot: e.dma_start(
                    XS[slot], yscr[128 * tc:128 * tc + 128, :]),
                    waits=yslot_state[slot] + (ywrites_done if tc < NXS else []),
                    inc=YSTRM[slot], amt=16)

            if NXS == 4:
                issue_y(0)
                issue_y(1)
            for t0 in range(0, TC, 2):
                if NXS == 4:
                    if t0 + 2 < TC:
                        issue_y(t0 + 2)
                        issue_y(t0 + 3)
                else:
                    issue_y(t0)
                    issue_y(t0 + 1)
                stat_toks = []
                for tc in (t0, t0 + 1):
                    tvs, tas = stat_pair(S2[:, 1024 * tc:1024 * tc + 1024], C_SUM + tc, C_SQ + tc,
                                         S3[:, 0:1024], S3[:, 1024:2048],
                                         ffn_done[-1], extra=ffn_done[:-1])
                    stat_toks += [tvs, tas]
                ta_r, _ = ln_chain("A", t0, 2, stat_toks)
                for tc in (t0, t0 + 1):
                    ob = tc % NOB
                    O = Oap[ob]
                    t1 = E("vector", lambda e, tc=tc, O=O: e.tensor_scalar(
                        O, S2[:, 1024 * tc:1024 * tc + 1024],
                        CONST[:, C_NM + tc:C_NM + tc + 1],
                        CONST[:, C_RSTD + tc:C_RSTD + tc + 1],
                        ALU.add, ALU.mult),
                        waits=[ta_r] + o_state[ob], inc="V")
                    t2 = E("vector", lambda e, O=O, YT=XS[tc % NXS]: e.tensor_tensor(
                        O, O, YT, ALU.add),
                        waits=[ytoks[tc]], inc="V")
                    yslot_state[tc % NXS] = [t2]
                    if tc % NXS in (0, 1):
                        pool_state[tc % NXS] = [t2]
                    todma = E("sync", lambda e, tc=tc, O=O, n=n: e.dma_start(
                        outd[n, 128 * tc:128 * tc + 128, :], O),
                        waits=[t2], inc="dO%d" % (tc % 4), amt=16)
                    o_state[ob] = [todma]
            pool_state[2] = list(o_state[0])
            s5x_state = [[("P", b.cnt["P"])] for _ in range(NSL)]
            yreads_done = [("dYr0", b.cnt["dYr0"]), ("dYr1", b.cnt["dYr1"]),
                           ("dXg", b.cnt["dXg"]), ("dXh", b.cnt["dXh"])]
            prev_final_reads = ([("V", b.cnt["V"]), ("A", b.cnt["A"])]
                                + [("dO%d" % i, b.cnt["dO%d" % i]) for i in range(4)])

        for s in ("dO0", "dO1", "dO2", "dO3", "dY0", "dY1", "dY2", "dY3",
                  "dX0", "dX1", "dXa", "dXb", "dXc", "dXd",
                  "dXe", "dXf", "dXg", "dXh", "dYr0", "dYr1", "dC"):
            b.wait_only("sync", [(s, b.cnt[s])])
        b.wait_only("gpsimd", [("dW0", b.cnt["dW0"]), ("dW1", b.cnt["dW1"]),
                               ("dW2", b.cnt["dW2"])])

        block = ctx.enter_context(nc.Block())
        block.sync(lambda e: b.replay("sync", e, sems))
        block.gpsimd(lambda e: b.replay("gpsimd", e, sems))
        block.tensor(lambda e: b.replay("tensor", e, sems))
        block.scalar(lambda e: b.replay("scalar", e, sems))
        block.vector(lambda e: b.replay("vector", e, sems))
    return nc


_NC_CACHE = {}


def get_nc(nseq, S):
    key = (nseq, S)
    if key not in _NC_CACHE:
        _NC_CACHE[key] = build_nc(nseq, S)
    return _NC_CACHE[key]


def make_in_maps(x_full, weights, nseq, S, ncores):
    bf = ml_dtypes.bfloat16
    ident = np.eye(128, dtype=bf)
    maskt = np.tril(np.ones((128, 128), dtype=np.float32), -1).astype(bf)
    shared = {
        "wqt": np.ascontiguousarray(weights["Wq"].T),
        "wkt": np.ascontiguousarray(weights["Wk"].T),
        "wvt": np.ascontiguousarray(weights["Wv"].T),
        "wrt": np.ascontiguousarray(weights["W_r"].T),
        "wkft": np.ascontiguousarray(weights["W_k"].T),
        "wvft": np.ascontiguousarray(weights["W_v"].T),
        "bq": np.ascontiguousarray(weights["bq"].reshape(CC, 128, 1)),
        "bk": np.ascontiguousarray(weights["bk"].reshape(CC, 128, 1)),
        "bv": np.ascontiguousarray(weights["bv"].reshape(1, D)),
        "mur": np.ascontiguousarray(weights["mu_r"].reshape(CC, 128, 1)),
        "muk": np.ascontiguousarray(weights["mu_k"].reshape(CC, 128, 1)),
        "ident": ident,
        "maskt": maskt,
    }
    in_maps = []
    for c in range(ncores):
        m = dict(shared)
        m["x"] = np.ascontiguousarray(x_full[c * nseq:(c + 1) * nseq])
        in_maps.append(m)
    return in_maps


def kernel(L0_attn_x, L0_attn_Wq, L0_attn_bq, L0_attn_Wk, L0_attn_bk,
           L0_attn_Wv, L0_attn_bv, L0_ffn_W_r, L0_ffn_W_k, L0_ffn_W_v,
           L0_ffn_mu_r, L0_ffn_mu_k):
    x = np.asarray(L0_attn_x, dtype=np.float32)
    B, S, D_ = x.shape
    assert D_ == D and B % NCORES == 0
    nseq = B // NCORES
    weights = {
        "Wq": np.asarray(L0_attn_Wq, np.float32), "bq": np.asarray(L0_attn_bq, np.float32),
        "Wk": np.asarray(L0_attn_Wk, np.float32), "bk": np.asarray(L0_attn_bk, np.float32),
        "Wv": np.asarray(L0_attn_Wv, np.float32), "bv": np.asarray(L0_attn_bv, np.float32),
        "W_r": np.asarray(L0_ffn_W_r, np.float32), "W_k": np.asarray(L0_ffn_W_k, np.float32),
        "W_v": np.asarray(L0_ffn_W_v, np.float32),
        "mu_r": np.asarray(L0_ffn_mu_r, np.float32), "mu_k": np.asarray(L0_ffn_mu_k, np.float32),
    }
    nc = get_nc(nseq, S)
    in_maps = make_in_maps(x, weights, nseq, S, NCORES)
    res = run_bass_kernel_spmd(nc, in_maps, core_ids=list(range(NCORES)))
    out = np.concatenate([np.asarray(r["out"], np.float32) for r in res.results], axis=0)
    return out.reshape(B, S, D)
